# version 11
# baseline (speedup 1.0000x reference)
import numpy as np
import concourse.bass as bass
import concourse.mybir as mybir
from concourse.bass_utils import run_bass_kernel_spmd

F32 = mybir.dt.float32
BF16 = mybir.dt.bfloat16
AF = mybir.ActivationFunctionType
ALU = mybir.AluOpType
AX = mybir.AxisListType

B, L, D = 4, 2048, 4096
NH, CD, NBL = 16, 32, 8
HD = CD * NBL
NHC = NH // 2
T = L
TC = L // 2
KC = D // 128
NT = T // 128
NTC = TC // 128
EPS = 1e-5
E = 64
KTOP = 8

STREAMS = ("pe", "dve", "act", "pool", "sync")
DEFSEM = {"pe": "pe", "dve": "dve", "act": "act", "pool": "pool", "sync": "qsy"}
QSEMS = ("qsy", "qsc", "qgp", "cc")
ALLSEMS = ("pe", "dve", "act", "pool", "qsy", "qsc", "qgp", "cc")


class Prog:

    def __init__(self):
        self.items = {s: [] for s in STREAMS}
        self.cnt = {s: 0 for s in ALLSEMS}
        self.lastwait = {}
        self.wr = {}
        self.rd = {}
        self.nwaits = 0

    def _wait(self, stream, sem, val):
        key = (stream, sem)
        if self.lastwait.get(key, -1) >= val:
            return
        self.lastwait[key] = val
        self.nwaits += 1
        self.items[stream].append(("w", sem, [val]))

    def op(self, stream, fn, reads=(), writes=(), sem=None):
        semname = sem or DEFSEM[stream]
        amt = 1 if semname == "cc" else (16 if semname in QSEMS else 1)
        need = {}
        for sp, lo, hi in reads:
            for l2, h2, ev in self.wr.get(sp, ()):
                if l2 < hi and lo < h2:
                    k = ev[0]
                    if ev[1] > need.get(k, -1):
                        need[k] = ev[1]
        for sp, lo, hi in writes:
            for l2, h2, ev in self.wr.get(sp, ()):
                if l2 < hi and lo < h2:
                    k = ev[0]
                    if ev[1] > need.get(k, -1):
                        need[k] = ev[1]
            for (l2, h2, s2), v2 in self.rd.get(sp, {}).items():
                if l2 < hi and lo < h2:
                    if v2 > need.get(s2, -1):
                        need[s2] = v2
        for s2 in sorted(need):
            if stream == "pe" and s2 == "pe":
                continue
            self._wait(stream, s2, need[s2])
        self.cnt[semname] += amt
        ev = (semname, self.cnt[semname])

        if semname == "cc":
            def run(e, ctx, fn=fn, s=semname):
                fn(e, ctx).then_inc(ctx[s])
        else:
            def run(e, ctx, fn=fn, s=semname, a=amt):
                fn(e, ctx).then_inc(ctx[s], a)
        self.items[stream].append(("op", run, semname if semname in ("qsy", "qsc", "qgp") else None))
        for sp, lo, hi in writes:
            wl = self.wr.setdefault(sp, [])
            wl[:] = [w for w in wl if not (w[0] >= lo and w[1] <= hi)]
            wl.append((lo, hi, ev))
            rl = self.rd.get(sp)
            if rl:
                for k in [k for k in rl if k[0] >= lo and k[1] <= hi]:
                    del rl[k]
        for sp, lo, hi in reads:
            rl = self.rd.setdefault(sp, {})
            k = (lo, hi, ev[0])
            if ev[1] > rl.get(k, -1):
                rl[k] = ev[1]
        return ev

    def finalize(self):
        wanted = {q: set() for q in ("qsy", "qsc", "qgp")}
        for s in STREAMS:
            for it in self.items[s]:
                if it[0] == "w" and it[1] in wanted:
                    wanted[it[1]].add(it[2][0])
        for q, s in (("qsy", "sync"), ("qsc", "act"), ("qgp", "pool")):
            out = []
            cum = 0
            for it in self.items[s]:
                out.append(it)
                if it[0] == "op" and it[2] == q:
                    cum += 16
                    if cum in wanted[q]:
                        out.append(("w", q, [cum]))
            self.items[s] = out
        ops = {s: [] for s in STREAMS}
        for s in STREAMS:
            for it in self.items[s]:
                if it[0] == "w":
                    def w(e, ctx, sem=it[1], holder=it[2]):
                        e.wait_ge(ctx[sem], holder[0])
                    ops[s].append(w)
                else:
                    ops[s].append(it[1])
        return ops


def _build_tables(g):
    dim = len(g)
    n = 2 ** dim
    masks = sorted(range(n), key=lambda m: (bin(m).count('1'), m))
    inv = {m: i for i, m in enumerate(masks)}
    C = np.zeros((n, n, n), dtype=np.float32)
    for i, a in enumerate(masks):
        for j, b in enumerate(masks):
            sw, at = (0, a >> 1)
            while at:
                sw += bin(at & b).count('1')
                at >>= 1
            s = -1.0 if sw & 1 else 1.0
            for t in range(dim):
                if a >> t & 1 and b >> t & 1:
                    s *= float(g[t])
            C[i, j, inv[a ^ b]] = s
    grades = np.array([bin(m).count('1') for m in masks])
    rev = np.where(grades * (grades - 1) // 2 % 2 == 1, -1.0, 1.0).astype(np.float32)
    met = np.array([float(np.prod([g[t] for t in range(dim) if m >> t & 1] + [1.0]))
                    for m in masks], dtype=np.float32)
    return C, rev, met


def _host_constants():
    CC, rev, met = _build_tables(np.array([1.0, 1.0, 1.0], dtype=np.float32))
    biv = CC[:, :, [4, 5, 6]]
    scale = 1.0 / np.sqrt(CD * NBL)
    pat = np.zeros((4, 128, 128), dtype=np.float32)
    for d in range(16):
        for c in range(3):
            pat[c, 8 * d:8 * d + 8, 8 * d:8 * d + 8] = rev[:, None] * biv[:, :, c]
        pat[3, 8 * d:8 * d + 8, 8 * d:8 * d + 8] = np.diag(rev * met * scale)
    bdp = np.tile(pat, (1, 1, NHC))
    return np.ascontiguousarray(bdp)


def build_nc():
    p = Prog()
    nc = bass.Bass("TRN2", target_bir_lowering=False)

    x_d = nc.dram_tensor("x", [T, D], F32, kind="ExternalInput").ap()
    xh_d = nc.dram_tensor("xh", [TC, D], F32, kind="ExternalInput").ap()
    wpk_d = nc.dram_tensor("wpk", [NHC, 3, 4, 128, 2048], F32, kind="ExternalInput").ap()
    bqp_d = nc.dram_tensor("bqp", [128, 16], F32, kind="ExternalInput").ap()
    bkp_d = nc.dram_tensor("bkp", [128, 16], F32, kind="ExternalInput").ap()
    bv_d = nc.dram_tensor("bv", [1, D], F32, kind="ExternalInput").ap()
    bdp_d = nc.dram_tensor("bdp", [4, 128, 1024], F32, kind="ExternalInput").ap()
    bws_d = nc.dram_tensor("bws", [3, 128, 1024], F32, kind="ExternalInput").ap()
    wrp_d = nc.dram_tensor("wrp", [128, KC * E], F32, kind="ExternalInput").ap()
    rbv_d = nc.dram_tensor("rbv", [1, E], F32, kind="ExternalInput").ap()
    ebm_d = nc.dram_tensor("ebm", [E, D], F32, kind="ExternalInput").ap()
    idm_d = nc.dram_tensor("idm", [128, 128], F32, kind="ExternalInput").ap()
    ones_d = nc.dram_tensor("onesv", [1, 128], F32, kind="ExternalInput").ap()
    msk_d = nc.dram_tensor("msk", [128, 2], F32, kind="ExternalInput").ap()
    y_d = nc.dram_tensor("y", [TC, D], F32, kind="ExternalOutput").ap()

    cin_d = [nc.dram_tensor(f"cin{g}", [T, 2 * HD], BF16, kind="Internal").ap() for g in range(4)]
    cout_d = [nc.dram_tensor(f"cout{g}", [2 * T, 2 * HD], BF16, kind="Internal").ap() for g in range(4)]

    def rdram(name, r0, r1):
        return ("d:" + name, r0, r1)

    arena = nc.sbuf_tensor("arena", [128, 209024], mybir.dt.uint8)
    arena_h = arena.__enter__()
    base = nc.lookup_mloc(arena_h).addr

    SBMAP = {}
    off = [0]

    def at(name, shape, dtype, o=None):
        if o is None:
            o = off[0]
        else:
            off[0] = o
        h = nc.alloc_sbuf_tensor_at(name, shape, dtype, offset=base + o)
        esz = mybir.dt.size(dtype)
        ncols = int(np.prod(shape[1:]))
        SBMAP[name] = (o, esz, ncols)
        off[0] = o + ((ncols * esz + 31) // 32) * 32
        return h.ap()

    def rb(name, c0=None, c1=None):
        o, esz, ncols = SBMAP[name]
        if c0 is None:
            c0, c1 = 0, ncols
        return ("sb", o + c0 * esz, o + c1 * esz)

    ln = at("ln", [128, KC * T], BF16, 0)
    wsl = [at(f"wsl{i}", [128, 2048], BF16) for i in range(6)]
    BDt = at("BD", [128, NHC * 128], BF16)
    identb = at("identb", [128, 128], BF16)
    bqps = at("bqps", [128, 16], F32)
    bkps = at("bkps", [128, 16], F32)
    sv = at("sv", [128, 40], F32)
    stats = at("stats", [128, 48], F32)
    statsB = at("statsB", [128, 48], F32)
    onesb = at("onesb", [1, 128], F32)
    mskb = at("mskb", [128, 2], F32)
    U0 = ((off[0] + 31) // 32) * 32
    xb = [at(f"xb{i}", [128, 2048], F32, U0 + i * 8192) for i in range(3)]
    lntmp = [at(f"lntmp{i}", [128, 2048], BF16, U0 + 24576 + i * 4096) for i in range(2)]
    Qb = at("Qb", [128, 2 * T], BF16, U0 + 0)
    Kb = at("Kb", [128, 2 * T], BF16, U0 + 8192)
    Vb = at("Vb", [128, 16 * HD], BF16, U0 + 16384)
    Pb = [at(f"Pb{i}", [128, T], BF16, U0 + 24576 + i * 4096) for i in range(2)]
    Ptb = at("Ptb", [128, T], BF16, U0 + 32768)
    Ohb = at("Ohb", [128, 16 * HD], BF16, U0 + 36864)
    bdsA = at("bdsA", [128, 1024], BF16, U0 + 32768)
    bdsB = at("bdsB", [128, 1024], BF16, U0 + 34816)
    bdtmp = at("bdtmp", [128, 1024], BF16, U0 + 36864)
    bvsb = at("bvsb", [1, D], F32, U0 + 0)
    r1b = [at(f"r1b{i}", [128, D], F32, U0 + 16384 + i * 16384) for i in range(2)]
    rbs = at("rbs", [1, E], F32, U0 + 49152)
    xc = [at(f"xc{i}", [128, D], F32, i * 16384) for i in range(2)]
    ab = [[at(f"ab{pr}_{i}", [128, 2048], BF16, 32768 + (pr * 4 + i) * 4096)
           for i in range(4)] for pr in range(2)]
    abP = [at(f"abP{pr}", [128, 8192], BF16, 32768 + pr * 16384) for pr in range(2)]
    off[0] = 65536
    ln2tm = [at(f"ln2tm{i}", [128, D], BF16, 65536 + i * 8192) for i in range(2)]
    ln2fm = [at(f"ln2fm{i}", [128, KC * 128], BF16, 81920 + i * 8192) for i in range(2)]
    bvB = at("bvB", [128, D], F32, 98304)
    wrb = at("wrb", [128, KC * E], BF16, 114688)
    ebb = at("ebb", [E, D], BF16, 118784)
    brB = at("brB", [128, E], F32, 126976)
    scb, exb, swb, wdf, wdm, wdb, wdnT = [], [], [], [], [], [], []
    for i in range(2):
        o = 127232 + i * 1664
        scb.append(at(f"scb{i}", [128, E], F32, o))
        exb.append(at(f"exb{i}", [128, E], F32, o + 256))
        swb.append(at(f"swb{i}", [128, E], F32, o + 512))
        wdf.append(at(f"wdf{i}", [128, E], F32, o + 768))
        wdm.append(at(f"wdm{i}", [128, E], F32, o + 1024))
        wdb.append(at(f"wdb{i}", [128, E], BF16, o + 1280))
        wdnT.append(at(f"wdnT{i}", [E, 128], BF16, o + 1408))

    MEAN, VAR, RSTD, NB_, LNV = 0, 1, 2, 3, 4
    NM0, NM1, NMF = 5, 6, 7
    SU00, SU01, SU10, SU11 = 8, 9, 10, 11
    SSUM, RS0, RS1 = 12, 13, 14
    M8, SE, RSE, NMX, WS = 15, 16, 17, 18, 19

    def svc(i):
        return sv[:, i:i + 1]

    def rsv(i):
        return rb("sv", i, i + 1)

    Tps_h = nc.psum_tensor("Tps", [128, 1024], BF16)
    PAB_h = nc.psum_tensor("PAB", [128, 1024], F32)
    Sps_h = nc.psum_tensor("Sps", [128, 2048], F32)
    PV_h = nc.psum_tensor("PV", [128, 256], F32)
    Tps = Tps_h.__enter__().ap()
    PAB = PAB_h.__enter__().ap()
    PA = PAB[:, 0:512]
    PB = PAB[:, 512:1024]
    Sps = Sps_h.__enter__().ap()
    PVp = PV_h.__enter__().ap()
    PROJ = [PA, PB]
    S3AP = [Sps[:, 0:1024], Sps[:, 1024:2048], PAB[:, 0:1024]]
    S3RG = [("ps", 6144, 10240), ("ps", 10240, 14336), ("ps", 2048, 6144)]
    def rpT(c0=0, c1=1024):
        return ("ps", 0 + c0 * 2, 0 + c1 * 2)

    def rpA(c0=0, c1=512):
        return ("ps", 2048 + c0 * 4, 2048 + c1 * 4)

    def rpB(c0=0, c1=512):
        return ("ps", 4096 + c0 * 4, 4096 + c1 * 4)

    def rpBank(g, c0=0, c1=512):
        return rpA(c0, c1) if g % 2 == 0 else rpB(c0, c1)

    def rpS(c0=0, c1=2048):
        return ("ps", 6144 + c0 * 4, 6144 + c1 * 4)

    def rpV(c0=0, c1=256):
        return ("ps", 14336 + c0 * 4, 14336 + c1 * 4)

    p.op("sync", lambda e, c: e.dma_start(out=bqps[:, :], in_=bqp_d[:, :]),
         reads=[rdram("bqp", 0, 128)], writes=[rb("bqps")])
    p.op("sync", lambda e, c: e.dma_start(out=bkps[:, :], in_=bkp_d[:, :]),
         reads=[rdram("bkp", 0, 128)], writes=[rb("bkps")])
    p.op("sync", lambda e, c: e.dma_start(out=mskb[:, :], in_=msk_d[:, :]),
         reads=[rdram("msk", 0, 128)], writes=[rb("mskb")])
    p.op("sync", lambda e, c: e.dma_start(out=onesb[:, :], in_=ones_d[:, :]),
         reads=[rdram("onesv", 0, 1)], writes=[rb("onesb")])
    p.op("pool", lambda e, c: e.dma_start(out=identb[:, :], in_=idm_d[:, :]),
         reads=[rdram("idm", 0, 128)], writes=[rb("identb")], sem="qgp")
    p.op("pool", lambda e, c: e.dma_start(out=BDt[:, :], in_=bdp_d[3]),
         reads=[rdram("bdp", 3, 4)], writes=[rb("BD")], sem="qgp")
    for cch in range(3):
        p.op("pool", lambda e, c, cch=cch: e.dma_start(out=bdsA[:, :], in_=bdp_d[cch]),
             reads=[rdram("bdp", cch, cch + 1)], writes=[rb("bdsA")], sem="qgp")
        p.op("pool", lambda e, c, cch=cch: e.dma_start(out=bdsB[:, :], in_=bws_d[cch]),
             reads=[rdram("bws", cch, cch + 1)], writes=[rb("bdsB")], sem="qgp")
        p.op("dve", lambda e, c: e.tensor_tensor(bdtmp[:, :], bdsA[:, :], bdsB[:, :], op=ALU.mult),
             reads=[rb("bdsA"), rb("bdsB")], writes=[rb("bdtmp")])
        p.op("dve", lambda e, c: e.tensor_tensor(BDt[:, :], BDt[:, :], bdtmp[:, :], op=ALU.add),
             reads=[rb("BD"), rb("bdtmp")], writes=[rb("BD")])

    for i in range(NT):
        for h in range(2):
            idx = 2 * i + h
            bufi = idx % 3
            p.op("sync",
                 lambda e, c, i=i, h=h, bufi=bufi: e.dma_start(
                     out=xb[bufi][:, :], in_=x_d[i * 128:(i + 1) * 128, h * 2048:(h + 1) * 2048]),
                 reads=[rdram("x", i * 128, (i + 1) * 128)], writes=[rb(f"xb{bufi}")])
            for cch in range(4):
                p.op("dve",
                     lambda e, c, bufi=bufi, h=h, cch=cch: e.bn_stats(
                         stats[:, (h * 4 + cch) * 6:(h * 4 + cch) * 6 + 6],
                         xb[bufi][:, cch * 512:(cch + 1) * 512]),
                     reads=[rb(f"xb{bufi}", cch * 512, (cch + 1) * 512)],
                     writes=[rb("stats", (h * 4 + cch) * 6, (h * 4 + cch) * 6 + 6)])
        p.op("dve", lambda e, c: e.bn_aggr(sv[:, MEAN:MEAN + 2], stats[:, :]),
             reads=[rb("stats")], writes=[rb("sv", MEAN, VAR + 1)])
        p.op("dve", lambda e, c: e.tensor_scalar(svc(LNV), svc(VAR), EPS, None, op0=ALU.add),
             reads=[rsv(VAR)], writes=[rsv(LNV)])
        p.op("act", lambda e, c: e.activation(svc(LNV), svc(LNV), AF.Ln),
             reads=[rsv(LNV)], writes=[rsv(LNV)])
        p.op("act", lambda e, c: e.activation(svc(RSTD), svc(LNV), AF.Exp, scale=-0.5),
             reads=[rsv(LNV)], writes=[rsv(RSTD)])
        p.op("dve", lambda e, c: e.tensor_scalar(
            svc(NB_), svc(MEAN), svc(RSTD), -1.0, op0=ALU.mult, op1=ALU.mult),
            reads=[rsv(MEAN), rsv(RSTD)], writes=[rsv(NB_)])
        for h in range(2):
            idx = 2 * i + h
            bufi = idx % 3
            par = idx % 2
            p.op("act",
                 lambda e, c, bufi=bufi, par=par: e.activation(
                     lntmp[par][:, :], xb[bufi][:, :], AF.Identity,
                     bias=svc(NB_), scale=svc(RSTD)),
                 reads=[rb(f"xb{bufi}"), rsv(NB_), rsv(RSTD)],
                 writes=[rb(f"lntmp{par}")])
            for g in range(2):
                def trgroup(e, c, par=par, g=g):
                    inst = None
                    for j in range(8):
                        inst = e.transpose(
                            Tps[:, j * 128:(j + 1) * 128],
                            lntmp[par][:, (g * 8 + j) * 128:(g * 8 + j) * 128 + 128],
                            identb[:, :])
                    return inst
                p.op("pe", trgroup,
                     reads=[rb(f"lntmp{par}", g * 1024, (g + 1) * 1024), rb("identb")],
                     writes=[rpT()])
                kc0 = h * 16 + g * 8
                p.op("act",
                     lambda e, c, kc0=kc0, i=i: e.copy(
                         ln.rearrange("p (kc t) -> p kc t", t=T)[:, kc0:kc0 + 8, i * 128:(i + 1) * 128],
                         Tps[:, 0:1024].rearrange("p (kc t) -> p kc t", t=128)),
                     reads=[rpT()], writes=[rb("ln")])

    def slot_of(w):
        return w % 6

    def load_piece(w):
        if w >= NHC * 12:
            return
        h, rem = divmod(w, 12)
        pr, m = divmod(rem, 4)
        p.op("pool",
             lambda e, c, h=h, pr=pr, m=m, w=w: e.dma_start(
                 out=wsl[slot_of(w)][:, :], in_=wpk_d[h, pr, m]),
             reads=[rdram("wpk", w, w + 1)], writes=[rb(f"wsl{slot_of(w)}")], sem="qgp")

    for w in range(6):
        load_piece(w)

    gidx = [0]
    for h in range(NHC):
        for pr in (0, 1):
            for fc in range(2):
                for tc in range(4):
                    g = gidx[0]
                    bank = PROJ[g % 2]

                    def mmgroup(e, c, h=h, pr=pr, fc=fc, tc=tc, bank=bank):
                        inst = None
                        for m in range(4):
                            ws = wsl[slot_of(h * 12 + pr * 4 + m)]
                            for l in range(8):
                                kc = m * 8 + l
                                inst = e.matmul(
                                    bank[:, :],
                                    ws[:, l * 256 + fc * 128: l * 256 + fc * 128 + 128],
                                    ln[:, kc * T + tc * 512: kc * T + tc * 512 + 512],
                                    start=(kc == 0), stop=(kc == 31))
                        return inst
                    p.op("pe", mmgroup,
                         reads=[rb(f"wsl{slot_of(h * 12 + pr * 4 + m)}") for m in range(4)] + [rb("ln")],
                         writes=[rpBank(g)])
                    dstb = Qb if pr == 0 else Kb
                    dstn = "Qb" if pr == 0 else "Kb"
                    biasv, biasn = (bqps, "bqps") if pr == 0 else (bkps, "bkps")
                    if g % 2 == 0:
                        fnc = (lambda e, c, dstb=dstb, biasv=biasv, fc=fc, tc=tc, h=h, bank=bank:
                               e.activation(
                                   dstb[:, fc * T + tc * 512: fc * T + tc * 512 + 512],
                                   bank[:, :], AF.Identity,
                                   bias=biasv[:, h * 2 + fc: h * 2 + fc + 1], scale=1.0))
                        eng = "act"
                    else:
                        fnc = (lambda e, c, dstb=dstb, biasv=biasv, fc=fc, tc=tc, h=h, bank=bank:
                               e.tensor_scalar(
                                   dstb[:, fc * T + tc * 512: fc * T + tc * 512 + 512],
                                   bank[:, :], biasv[:, h * 2 + fc: h * 2 + fc + 1], None,
                                   op0=ALU.add))
                        eng = "dve"
                    p.op(eng, fnc,
                         reads=[rpBank(g), rb(biasn, h * 2 + fc, h * 2 + fc + 1)],
                         writes=[rb(dstn, fc * T + tc * 512, fc * T + tc * 512 + 512)])
                    gidx[0] += 1
            for m in range(4):
                load_piece(h * 12 + pr * 4 + m + 6)

        for tc16 in range(16):
            g = gidx[0]
            bank = PROJ[g % 2]

            def vgroup(e, c, h=h, tc16=tc16, bank=bank):
                inst = None
                for m in range(4):
                    ws = wsl[slot_of(h * 12 + 8 + m)]
                    for l in range(8):
                        kc = m * 8 + l
                        inst = e.matmul(
                            bank[:, 0:256],
                            ln[:, kc * T + tc16 * 128: kc * T + tc16 * 128 + 128],
                            ws[:, l * 256: l * 256 + 256],
                            start=(kc == 0), stop=(kc == 31))
                return inst
            p.op("pe", vgroup,
                 reads=[rb(f"wsl{slot_of(h * 12 + 8 + m)}") for m in range(4)] + [rb("ln")],
                 writes=[rpBank(g, 0, 256)])
            if g % 2 == 0:
                fnc = (lambda e, c, tc16=tc16, bank=bank: e.copy(
                    Vb[:, tc16 * 256: tc16 * 256 + 256], bank[:, 0:256]))
                eng = "act"
            else:
                fnc = (lambda e, c, tc16=tc16, bank=bank: e.tensor_copy(
                    Vb[:, tc16 * 256: tc16 * 256 + 256], bank[:, 0:256]))
                eng = "dve"
            p.op(eng, fnc,
                 reads=[rpBank(g, 0, 256)],
                 writes=[rb("Vb", tc16 * 256, tc16 * 256 + 256)])
            gidx[0] += 1
        for m in range(4):
            load_piece(h * 12 + 8 + m + 6)

        for fc in range(2):
            for tc in range(4):
                g = gidx[0]
                bank = PROJ[g % 2]
                p.op("pe",
                     lambda e, c, h=h, fc=fc, tc=tc, bank=bank: e.matmul(
                         bank[:, :], BDt[:, h * 128:(h + 1) * 128],
                         Qb[:, fc * T + tc * 512: fc * T + tc * 512 + 512],
                         start=True, stop=True),
                     reads=[rb("BD", h * 128, (h + 1) * 128),
                            rb("Qb", fc * T + tc * 512, fc * T + tc * 512 + 512)],
                     writes=[rpBank(g)])
                if g % 2 == 0:
                    fnc = (lambda e, c, fc=fc, tc=tc, bank=bank: e.activation(
                        Qb[:, fc * T + tc * 512: fc * T + tc * 512 + 512],
                        bank[:, :], AF.Identity))
                    eng = "act"
                else:
                    fnc = (lambda e, c, fc=fc, tc=tc, bank=bank: e.tensor_copy(
                        Qb[:, fc * T + tc * 512: fc * T + tc * 512 + 512], bank[:, :]))
                    eng = "dve"
                p.op(eng, fnc,
                     reads=[rpBank(g)],
                     writes=[rb("Qb", fc * T + tc * 512, fc * T + tc * 512 + 512)])
                gidx[0] += 1

        NMP = [(NM0, NM1, NMF), (32, 33, 34)]
        for stage in range(17):
            pq = stage - 1
            if pq >= 0:
                par = pq % 2
                RSc = RS0 if par == 0 else RS1
                for hf in range(2):
                    def ptgroup(e, c, par=par, hf=hf):
                        inst = None
                        for j in range(8):
                            inst = e.transpose(
                                Tps[:, j * 128:(j + 1) * 128],
                                Pb[par][:, hf * 1024 + j * 128: hf * 1024 + j * 128 + 128],
                                identb[:, :])
                        return inst
                    p.op("pe", ptgroup,
                         reads=[rb(f"Pb{par}", hf * 1024, (hf + 1) * 1024), rb("identb")],
                         writes=[rpT()])
                    if hf == 0:
                        fnc = (lambda e, c, hf=hf: e.tensor_copy(
                            Ptb[:, hf * 1024: hf * 1024 + 1024], Tps[:, 0:1024]))
                        eng = "dve"
                    else:
                        fnc = (lambda e, c, hf=hf: e.copy(
                            Ptb[:, hf * 1024: hf * 1024 + 1024], Tps[:, 0:1024]))
                        eng = "act"
                    p.op(eng, fnc,
                         reads=[rpT()], writes=[rb("Ptb", hf * 1024, (hf + 1) * 1024)])

                def pvmm(e, c):
                    inst = None
                    for kc in range(16):
                        inst = e.matmul(PVp[:, :], Ptb[:, kc * 128:(kc + 1) * 128],
                                        Vb[:, kc * 256:(kc + 1) * 256],
                                        start=(kc == 0), stop=(kc == 15))
                    return inst
                p.op("pe", pvmm, reads=[rb("Ptb"), rb("Vb")], writes=[rpV()])
                p.op("dve",
                     lambda e, c, pq=pq, RSc=RSc: e.tensor_scalar(
                         Ohb[:, pq * 256:(pq + 1) * 256], PVp[:, :], svc(RSc), None, op0=ALU.mult),
                     reads=[rpV(), rsv(RSc)],
                     writes=[rb("Ohb", pq * 256, (pq + 1) * 256)])
            qt = stage
            if qt < 16:
                par = qt % 2
                SUc = (SU00, SU01) if par == 0 else (SU10, SU11)
                nm0, nm1, nmf = NMP[par]
                for hf in range(2):
                    sl = (2 * qt + hf) % 3
                    sap, srg = S3AP[sl], S3RG[sl]

                    def smm(e, c, qt=qt, hf=hf, sap=sap):
                        inst = None
                        for n2 in range(2):
                            for fc in range(2):
                                inst = e.matmul(
                                    sap[:, n2 * 512: n2 * 512 + 512],
                                    Qb[:, fc * T + qt * 128: fc * T + qt * 128 + 128],
                                    Kb[:, fc * T + hf * 1024 + n2 * 512: fc * T + hf * 1024 + n2 * 512 + 512],
                                    start=(fc == 0), stop=(fc == 1))
                        return inst
                    p.op("pe", smm,
                         reads=[rb("Qb", qt * 128, qt * 128 + 128),
                                rb("Qb", T + qt * 128, T + qt * 128 + 128),
                                rb("Kb", hf * 1024, hf * 1024 + 1024),
                                rb("Kb", T + hf * 1024, T + hf * 1024 + 1024)],
                         writes=[srg])
                    nmc = nm0 if hf == 0 else nm1
                    p.op("dve", lambda e, c, sap=sap, nmc=nmc: e.tensor_reduce(
                        svc(nmc), sap[:, :], axis=AX.X, op=ALU.max, negate=True),
                        reads=[srg], writes=[rsv(nmc)])
                p.op("dve", lambda e, c, nm0=nm0, nm1=nm1, nmf=nmf: e.tensor_tensor(
                    svc(nmf), svc(nm0), svc(nm1), op=ALU.min),
                    reads=[rsv(nm0), rsv(nm1)], writes=[rsv(nmf)])
                for hf in range(2):
                    sl = (2 * qt + hf) % 3
                    sap, srg = S3AP[sl], S3RG[sl]
                    su = SUc[hf]
                    p.op("act",
                         lambda e, c, par=par, hf=hf, su=su, sap=sap, nmf=nmf: e.activation(
                             Pb[par][:, hf * 1024: hf * 1024 + 1024],
                             sap[:, :],
                             AF.Exp, bias=svc(nmf), scale=1.0, accum_out=svc(su)),
                         reads=[srg, rsv(nmf)],
                         writes=[rb(f"Pb{par}", hf * 1024, (hf + 1) * 1024), rsv(su)])
                RSc = RS0 if par == 0 else RS1
                p.op("dve", lambda e, c, SUc=SUc: e.tensor_tensor(
                    svc(SSUM), svc(SUc[0]), svc(SUc[1]), op=ALU.add),
                    reads=[rsv(SUc[0]), rsv(SUc[1])], writes=[rsv(SSUM)])
                p.op("dve", lambda e, c, RSc=RSc: e.reciprocal(svc(RSc), svc(SSUM)),
                     reads=[rsv(SSUM)], writes=[rsv(RSc)])
        grp = h // 2
        p.op("sync",
             lambda e, c, h=h, grp=grp: e.dma_start(
                 out=cin_d[grp][:, (h % 2) * HD:(h % 2 + 1) * HD].rearrange("(qt p) f -> p qt f", p=128),
                 in_=Ohb[:, :].rearrange("p (qt f) -> p qt f", f=HD)),
             reads=[rb("Ohb")], writes=[rdram(f"cin{grp}", 0, T)])
        if h % 2 == 1:
            p.op("pool", lambda e, c, grp=grp: e.collective_compute(
                "AllGather", ALU.bypass,
                replica_groups=[[0, 1], [2, 3], [4, 5], [6, 7]],
                ins=[cin_d[grp]], outs=[cout_d[grp]]),
                reads=[rdram(f"cin{grp}", 0, T)],
                writes=[rdram(f"cout{grp}", 0, 2 * T)], sem="cc")

    p.op("pool", lambda e, c: e.dma_start(out=wrb[:, :], in_=wrp_d[:, :]),
         reads=[rdram("wrp", 0, 128)], writes=[rb("wrb")], sem="qgp")
    p.op("pool", lambda e, c: e.dma_start(out=ebb[:, :], in_=ebm_d[:, :]),
         reads=[rdram("ebm", 0, E)], writes=[rb("ebb")], sem="qgp")
    p.op("sync", lambda e, c: e.dma_start(out=bvsb[:, :], in_=bv_d[:, :]),
         reads=[rdram("bv", 0, 1)], writes=[rb("bvsb")])
    p.op("sync", lambda e, c: e.dma_start(out=rbs[:, :], in_=rbv_d[:, :]),
         reads=[rdram("rbv", 0, 1)], writes=[rb("rbs")])
    for j in range(8):
        bank = PROJ[j % 2]
        p.op("pe", lambda e, c, j=j, bank=bank: e.matmul(
            bank[:, :], onesb[0:1, :], bvsb[0:1, j * 512:(j + 1) * 512],
            start=True, stop=True),
            reads=[rb("onesb"), rb("bvsb", j * 512, (j + 1) * 512)],
            writes=[rpBank(j)])
        p.op("dve", lambda e, c, j=j, bank=bank: e.tensor_copy(
            bvB[:, j * 512:(j + 1) * 512], bank[:, :]),
            reads=[rpBank(j)], writes=[rb("bvB", j * 512, (j + 1) * 512)])
    p.op("pe", lambda e, c: e.matmul(PA[:, 0:64], onesb[0:1, :], rbs[0:1, :],
                                     start=True, stop=True),
         reads=[rb("onesb"), rb("rbs")], writes=[rpA(0, 64)])
    p.op("dve", lambda e, c: e.tensor_copy(brB[:, :], PA[:, 0:64]),
         reads=[rpA(0, 64)], writes=[rb("brB")])

    SVP = [{"MEAN": 0, "VAR": 1, "RSTD": 2, "NB": 3, "LNV": 4,
            "M8": 15, "SE": 16, "RSE": 17, "NMX": 18, "WS": 19},
           {"MEAN": 20, "VAR": 21, "RSTD": 22, "NB": 23, "LNV": 24,
            "M8": 25, "SE": 26, "RSE": 27, "NMX": 28, "WS": 29}]
    statsP = [stats, statsB]
    statsN = ["stats", "statsB"]
    for t in range(NTC):
        par = t % 2
        C = SVP[par]
        stt, stn = statsP[par], statsN[par]
        abhull = ("sb", 32768 + par * 16384, 32768 + (par + 1) * 16384)

        def sva(k, C=C):
            return svc(C[k])

        def rsva(k, C=C):
            return rsv(C[k])
        p.op("sync", lambda e, c, t=t, par=par: e.dma_start(
            out=xc[par][:, :], in_=xh_d[t * 128:(t + 1) * 128, :]),
            reads=[rdram("xh", t * 128, (t + 1) * 128)], writes=[rb(f"xc{par}")])
        for grp in range(4):
            p.op("pool", lambda e, c, par=par, t=t, grp=grp: e.dma_start(
                out=abP[par].rearrange("p (i f) -> p i f", f=2048)[:, :, grp * 512:(grp + 1) * 512],
                in_=cout_d[grp].rearrange("(i tt r) f -> tt r i f", i=4, tt=8, r=128)[t]),
                reads=[rdram(f"cout{grp}", 0, 2 * T)],
                writes=[("sb", 32768 + par * 16384 + grp * 1024,
                         32768 + par * 16384 + 16384)], sem="qgp")
        r1 = r1b[par]
        r1n = f"r1b{par}"
        for half in range(2):
            a0 = ab[par][2 * half + 0]
            a1 = ab[par][2 * half + 1]
            p.op("dve", lambda e, c, half=half, a0=a0, par=par, r1=r1: e.scalar_tensor_tensor(
                r1[:, half * 2048:(half + 1) * 2048], a0[:, :], mskb[:, 0:1],
                xc[par][:, half * 2048:(half + 1) * 2048], op0=ALU.mult, op1=ALU.add),
                reads=[abhull, rb("mskb", 0, 1), rb(f"xc{par}", half * 2048, (half + 1) * 2048)],
                writes=[rb(r1n, half * 2048, (half + 1) * 2048)])
            p.op("dve", lambda e, c, half=half, a1=a1, r1=r1: e.scalar_tensor_tensor(
                r1[:, half * 2048:(half + 1) * 2048], a1[:, :], mskb[:, 1:2],
                r1[:, half * 2048:(half + 1) * 2048], op0=ALU.mult, op1=ALU.add),
                reads=[abhull, rb("mskb", 1, 2), rb(r1n, half * 2048, (half + 1) * 2048)],
                writes=[rb(r1n, half * 2048, (half + 1) * 2048)])
            p.op("dve", lambda e, c, half=half, r1=r1: e.tensor_tensor(
                r1[:, half * 2048:(half + 1) * 2048],
                r1[:, half * 2048:(half + 1) * 2048],
                bvB[:, half * 2048:(half + 1) * 2048], op=ALU.add),
                reads=[rb(r1n, half * 2048, (half + 1) * 2048),
                       rb("bvB", half * 2048, (half + 1) * 2048)],
                writes=[rb(r1n, half * 2048, (half + 1) * 2048)])
        for cch in range(8):
            p.op("dve", lambda e, c, cch=cch, r1=r1, stt=stt: e.bn_stats(
                stt[:, cch * 6:cch * 6 + 6], r1[:, cch * 512:(cch + 1) * 512]),
                reads=[rb(r1n, cch * 512, (cch + 1) * 512)],
                writes=[rb(stn, cch * 6, cch * 6 + 6)])
        p.op("dve", lambda e, c, stt=stt: e.bn_aggr(sv[:, C["MEAN"]:C["MEAN"] + 2], stt[:, :]),
             reads=[rb(stn)], writes=[rb("sv", C["MEAN"], C["VAR"] + 1)])
        p.op("dve", lambda e, c: e.tensor_scalar(sva("LNV"), sva("VAR"), EPS, None, op0=ALU.add),
             reads=[rsva("VAR")], writes=[rsva("LNV")])
        p.op("act", lambda e, c: e.activation(sva("LNV"), sva("LNV"), AF.Ln),
             reads=[rsva("LNV")], writes=[rsva("LNV")])
        p.op("act", lambda e, c: e.activation(sva("RSTD"), sva("LNV"), AF.Exp, scale=-0.5),
             reads=[rsva("LNV")], writes=[rsva("RSTD")])
        p.op("dve", lambda e, c: e.tensor_scalar(sva("NB"), sva("MEAN"), -1.0, None, op0=ALU.mult),
             reads=[rsva("MEAN")], writes=[rsva("NB")])
        lt = ln2tm[par]
        ltn = f"ln2tm{par}"
        lf = ln2fm[par]
        lfn = f"ln2fm{par}"
        for half in range(2):
            p.op("act", lambda e, c, half=half, r1=r1, lt=lt: e.activation(
                lt[:, half * 2048:(half + 1) * 2048],
                r1[:, half * 2048:(half + 1) * 2048],
                AF.Identity, bias=sva("NB"), scale=sva("RSTD")),
                reads=[rb(r1n, half * 2048, (half + 1) * 2048), rsva("NB"), rsva("RSTD")],
                writes=[rb(ltn, half * 2048, (half + 1) * 2048)])
        for g in range(4):
            def tr2group(e, c, g=g, lt=lt):
                inst = None
                for j in range(8):
                    inst = e.transpose(
                        Tps[:, j * 128:(j + 1) * 128],
                        lt[:, (g * 8 + j) * 128:(g * 8 + j) * 128 + 128],
                        identb[:, :])
                return inst
            p.op("pe", tr2group,
                 reads=[rb(ltn, g * 1024, (g + 1) * 1024), rb("identb")],
                 writes=[rpT()])
            eng = "dve" if g % 2 == 0 else "act"
            fnc = (lambda e, c, g=g, lf=lf: e.tensor_copy(
                lf.rearrange("p (kc t1) -> p kc t1", t1=128)[:, g * 8:(g + 1) * 8, :],
                Tps[:, 0:1024].rearrange("p (kc t1) -> p kc t1", t1=128))) if eng == "dve" else (
                lambda e, c, g=g, lf=lf: e.copy(
                    lf.rearrange("p (kc t1) -> p kc t1", t1=128)[:, g * 8:(g + 1) * 8, :],
                    Tps[:, 0:1024].rearrange("p (kc t1) -> p kc t1", t1=128)))
            p.op(eng, fnc, reads=[rpT()], writes=[rb(lfn)])
        def rmm(e, c, lf=lf):
            inst = None
            for kc in range(KC):
                inst = e.matmul(PA[:, 0:64], lf[:, kc * 128:(kc + 1) * 128],
                                wrb[:, kc * 64:(kc + 1) * 64],
                                start=(kc == 0), stop=(kc == 31))
            return inst
        p.op("pe", rmm, reads=[rb(lfn), rb("wrb")], writes=[rpA(0, 64)])
        sc, scn = scb[par], f"scb{par}"
        ex, exn = exb[par], f"exb{par}"
        sw, swn = swb[par], f"swb{par}"
        wf, wfn = wdf[par], f"wdf{par}"
        wm, wmn = wdm[par], f"wdm{par}"
        wb_, wbn = wdb[par], f"wdb{par}"
        wt, wtn = wdnT[par], f"wdnT{par}"
        p.op("dve", lambda e, c, sc=sc: e.tensor_tensor(
            sc[:, :], PA[:, 0:64], brB[:, :], op=ALU.add),
            reads=[rpA(0, 64), rb("brB")], writes=[rb(scn)])
        p.op("dve", lambda e, c, sc=sc: e.tensor_reduce(
            sva("NMX"), sc[:, :], axis=AX.X, op=ALU.max, negate=True),
            reads=[rb(scn)], writes=[rsva("NMX")])
        p.op("act", lambda e, c, sc=sc, ex=ex: e.activation(
            ex[:, :], sc[:, :], AF.Exp, bias=sva("NMX"), scale=1.0, accum_out=sva("SE")),
            reads=[rb(scn), rsva("NMX")], writes=[rb(exn), rsva("SE")])
        p.op("dve", lambda e, c: e.reciprocal(sva("RSE"), sva("SE")),
             reads=[rsva("SE")], writes=[rsva("RSE")])
        p.op("dve", lambda e, c, sc=sc, ex=ex: e.tensor_scalar(
            sc[:, :], ex[:, :], sva("RSE"), None, op0=ALU.mult),
            reads=[rb(exn), rsva("RSE")], writes=[rb(scn)])
        p.op("dve", lambda e, c, sc=sc, sw=sw: e.tensor_copy(sw[:, :], sc[:, :]),
             reads=[rb(scn)], writes=[rb(swn)])
        for j in range(KTOP):
            p.op("dve", lambda e, c, sw=sw: e.tensor_reduce(
                sva("M8"), sw[:, :], axis=AX.X, op=ALU.max),
                reads=[rb(swn)], writes=[rsva("M8")])
            if j < KTOP - 1:
                p.op("dve", lambda e, c, sw=sw, wm=wm: e.tensor_scalar(
                    wm[:, :], sw[:, :], sva("M8"), -2.0, op0=ALU.is_ge, op1=ALU.mult),
                    reads=[rb(swn), rsva("M8")], writes=[rb(wmn)])
                p.op("dve", lambda e, c, sw=sw, wm=wm: e.tensor_tensor(
                    sw[:, :], sw[:, :], wm[:, :], op=ALU.add),
                    reads=[rb(swn), rb(wmn)], writes=[rb(swn)])
        p.op("dve", lambda e, c, sc=sc, wf=wf: e.scalar_tensor_tensor(
            wf[:, :], sc[:, :], sva("M8"), sc[:, :], op0=ALU.is_ge, op1=ALU.mult,
            accum_out=sva("WS")),
            reads=[rb(scn), rsva("M8")], writes=[rb(wfn), rsva("WS")])
        p.op("dve", lambda e, c, wf=wf, wb_=wb_: e.tensor_copy(wb_[:, :], wf[:, :]),
             reads=[rb(wfn)], writes=[rb(wbn)])
        p.op("pe", lambda e, c, wb_=wb_: e.transpose(
            Tps[0:64, 0:128], wb_[:, 0:64], identb[:, :]),
            reads=[rb(wbn), rb("identb")], writes=[rpT(0, 128)])
        p.op("act", lambda e, c, wt=wt: e.copy(wt[0:E, :], Tps[0:64, 0:128]),
             reads=[rpT(0, 128)], writes=[rb(wtn)])
        for j in range(8):
            bank = PROJ[(j + 1) % 2]
            p.op("pe", lambda e, c, j=j, bank=bank, wt=wt: e.matmul(
                bank[:, :], wt[0:E, :], ebb[0:E, j * 512:(j + 1) * 512],
                start=True, stop=True),
                reads=[rb(wtn), rb("ebb", j * 512, (j + 1) * 512)],
                writes=[rpBank(j + 1)])
            p.op("dve", lambda e, c, j=j, bank=bank, r1=r1: e.tensor_tensor(
                r1[:, j * 512:(j + 1) * 512], bank[:, :],
                r1[:, j * 512:(j + 1) * 512], op=ALU.add),
                reads=[rpBank(j + 1), rb(r1n, j * 512, (j + 1) * 512)],
                writes=[rb(r1n, j * 512, (j + 1) * 512)])
        p.op("dve", lambda e, c, r1=r1, lt=lt: e.scalar_tensor_tensor(
            r1[:, :], lt[:, :], sva("WS"), r1[:, :], op0=ALU.mult, op1=ALU.add),
            reads=[rb(ltn), rsva("WS"), rb(r1n)], writes=[rb(r1n)])
        p.op("sync", lambda e, c, t=t, par=par, r1=r1: e.dma_start(
            out=y_d[t * 128:(t + 1) * 128, :], in_=r1[:, :]),
            reads=[rb(r1n)], writes=[rdram("y", t * 128, (t + 1) * 128)])

    p.items["sync"].append(("w", "qsy", [p.cnt["qsy"]]))
    ops = p.finalize()

    with (
        nc.semaphore("s_pe") as s_pe,
        nc.semaphore("s_dve") as s_dve,
        nc.semaphore("s_act") as s_act,
        nc.semaphore("s_pool") as s_pool,
        nc.semaphore("s_qsy") as s_qsy,
        nc.semaphore("s_qsc") as s_qsc,
        nc.semaphore("s_qgp") as s_qgp,
        nc.semaphore("s_cc") as s_cc,
        nc.Block() as block,
    ):
        ctx = {"pe": s_pe, "dve": s_dve, "act": s_act, "pool": s_pool,
               "qsy": s_qsy, "qsc": s_qsc, "qgp": s_qgp, "cc": s_cc}

        @block.tensor
        def _(e):
            for f in ops["pe"]:
                f(e, ctx)

        @block.vector
        def _(e):
            for f in ops["dve"]:
                f(e, ctx)

        @block.scalar
        def _(e):
            for f in ops["act"]:
                f(e, ctx)

        @block.gpsimd
        def _(e):
            for f in ops["pool"]:
                f(e, ctx)

        @block.sync
        def _(e):
            for f in ops["sync"]:
                f(e, ctx)

    return nc


_NC_CACHE = None


def _get_nc():
    global _NC_CACHE
    if _NC_CACHE is None:
        _NC_CACHE = build_nc()
    return _NC_CACHE


def kernel(L0_attn_x, L0_attn_Wq, L0_attn_bq, L0_attn_Wk, L0_attn_bk, L0_attn_Wv, L0_attn_bv,
           L0_attn_bias_weight, L0_ffn_router_weight, L0_ffn_router_bias, L0_ffn_expert_bias):
    x = np.asarray(L0_attn_x, dtype=np.float32)
    Wq = np.asarray(L0_attn_Wq, dtype=np.float32)
    Wk = np.asarray(L0_attn_Wk, dtype=np.float32)
    Wv = np.asarray(L0_attn_Wv, dtype=np.float32)
    bq = np.asarray(L0_attn_bq, dtype=np.float32)
    bk = np.asarray(L0_attn_bk, dtype=np.float32)
    bv = np.asarray(L0_attn_bv, dtype=np.float32)
    bw = np.asarray(L0_attn_bias_weight, dtype=np.float32)
    Wr = np.asarray(L0_ffn_router_weight, dtype=np.float32)
    rb_ = np.asarray(L0_ffn_router_bias, dtype=np.float32)
    eb = np.asarray(L0_ffn_expert_bias, dtype=np.float32)

    bdp = _host_constants()
    ident = np.eye(128, dtype=np.float32)
    onesv = np.ones((1, 128), dtype=np.float32)
    wrp = np.ascontiguousarray(Wr.T.reshape(KC, 128, E).transpose(1, 0, 2).reshape(128, KC * E))

    def pack_w(W, s):
        WT = W.T[:, s * 2048:(s + 1) * 2048]
        out = np.empty((NHC, 4, 128, 2048), dtype=np.float32)
        for j in range(NHC):
            sl = WT[:, j * HD:(j + 1) * HD]
            a = sl.reshape(4, 8, 128, HD).transpose(0, 2, 1, 3)
            out[j] = a.reshape(4, 128, 8 * HD)
        return out

    in_maps = []
    for core in range(8):
        b, s = divmod(core, 2)
        wpk = np.stack([pack_w(Wq, s), pack_w(Wk, s), pack_w(Wv, s)], axis=1)
        bqs = bq[s * 2048:(s + 1) * 2048].reshape(NHC, 2, 128).transpose(2, 0, 1).reshape(128, 16)
        bks = bk[s * 2048:(s + 1) * 2048].reshape(NHC, 2, 128).transpose(2, 0, 1).reshape(128, 16)
        bws = np.broadcast_to(
            bw[s * NHC:(s + 1) * NHC].T.reshape(3, 1, NHC, 1), (3, 128, NHC, 128)
        ).reshape(3, 128, NHC * 128)
        msk = np.zeros((128, 2), dtype=np.float32)
        msk[:, s] = 1.0
        in_maps.append({
            "x": np.ascontiguousarray(x[b]),
            "xh": np.ascontiguousarray(x[b, s * TC:(s + 1) * TC]),
            "wpk": np.ascontiguousarray(wpk),
            "bqp": np.ascontiguousarray(bqs),
            "bkp": np.ascontiguousarray(bks),
            "bv": bv.reshape(1, D),
            "bdp": bdp,
            "bws": np.ascontiguousarray(bws),
            "wrp": wrp,
            "rbv": rb_.reshape(1, E),
            "ebm": eb,
            "idm": ident,
            "onesv": onesv,
            "msk": msk,
        })

    nc = _get_nc()
    res = run_bass_kernel_spmd(nc, in_maps, core_ids=list(range(8)))
    out = np.empty((B, L, D), dtype=np.float32)
    for core in range(8):
        b, s = divmod(core, 2)
        out[b, s * TC:(s + 1) * TC, :] = res.results[core]["y"]
    return out


# version 12
# speedup vs baseline: 1.2122x; 1.2122x over previous
import numpy as np
import concourse.bass as bass
import concourse.mybir as mybir
from concourse.bass_utils import run_bass_kernel_spmd

F32 = mybir.dt.float32
BF16 = mybir.dt.bfloat16
AF = mybir.ActivationFunctionType
ALU = mybir.AluOpType
AX = mybir.AxisListType

B, L, D = 4, 2048, 4096
NH, CD, NBL = 16, 32, 8
HD = CD * NBL
NHC = NH // 2
T = L
TC = L // 2
KC = D // 128
NT = T // 128
NTC = TC // 128
EPS = 1e-5
E = 64
KTOP = 8

STREAMS = ("pe", "dve", "act", "pool", "sync")
DEFSEM = {"pe": "pe", "dve": "dve", "act": "act", "pool": "pool", "sync": "qsy"}
QSEMS = ("qsy", "qsc", "qgp", "cc")
ALLSEMS = ("pe", "dve", "act", "pool", "qsy", "qsc", "qgp", "cc")


class Prog:

    def __init__(self):
        self.items = {s: [] for s in STREAMS}
        self.cnt = {s: 0 for s in ALLSEMS}
        self.lastwait = {}
        self.wr = {}
        self.rd = {}
        self.nwaits = 0

    def _wait(self, stream, sem, val):
        key = (stream, sem)
        if self.lastwait.get(key, -1) >= val:
            return
        self.lastwait[key] = val
        self.nwaits += 1
        self.items[stream].append(("w", sem, [val]))

    def op(self, stream, fn, reads=(), writes=(), sem=None):
        semname = sem or DEFSEM[stream]
        amt = 1 if semname == "cc" else (16 if semname in QSEMS else 1)
        need = {}
        for sp, lo, hi in reads:
            for l2, h2, ev in self.wr.get(sp, ()):
                if l2 < hi and lo < h2:
                    k = ev[0]
                    if ev[1] > need.get(k, -1):
                        need[k] = ev[1]
        for sp, lo, hi in writes:
            for l2, h2, ev in self.wr.get(sp, ()):
                if l2 < hi and lo < h2:
                    k = ev[0]
                    if ev[1] > need.get(k, -1):
                        need[k] = ev[1]
            for (l2, h2, s2), v2 in self.rd.get(sp, {}).items():
                if l2 < hi and lo < h2:
                    if v2 > need.get(s2, -1):
                        need[s2] = v2
        for s2 in sorted(need):
            if stream == "pe" and s2 == "pe":
                continue
            self._wait(stream, s2, need[s2])
        self.cnt[semname] += amt
        ev = (semname, self.cnt[semname])

        if semname == "cc":
            def run(e, ctx, fn=fn, s=semname):
                fn(e, ctx).then_inc(ctx[s])
        else:
            def run(e, ctx, fn=fn, s=semname, a=amt):
                fn(e, ctx).then_inc(ctx[s], a)
        self.items[stream].append(("op", run, semname if semname in ("qsy", "qsc", "qgp") else None))
        for sp, lo, hi in writes:
            wl = self.wr.setdefault(sp, [])
            wl[:] = [w for w in wl if not (w[0] >= lo and w[1] <= hi)]
            wl.append((lo, hi, ev))
            rl = self.rd.get(sp)
            if rl:
                for k in [k for k in rl if k[0] >= lo and k[1] <= hi]:
                    del rl[k]
        for sp, lo, hi in reads:
            rl = self.rd.setdefault(sp, {})
            k = (lo, hi, ev[0])
            if ev[1] > rl.get(k, -1):
                rl[k] = ev[1]
        return ev

    def finalize(self):
        wanted = {q: set() for q in ("qsy", "qsc", "qgp")}
        for s in STREAMS:
            for it in self.items[s]:
                if it[0] == "w" and it[1] in wanted:
                    wanted[it[1]].add(it[2][0])
        for q, s in (("qsy", "sync"), ("qsc", "act"), ("qgp", "pool")):
            out = []
            cum = 0
            for it in self.items[s]:
                out.append(it)
                if it[0] == "op" and it[2] == q:
                    cum += 16
                    if cum in wanted[q]:
                        out.append(("w", q, [cum]))
            self.items[s] = out
        ops = {s: [] for s in STREAMS}
        for s in STREAMS:
            for it in self.items[s]:
                if it[0] == "w":
                    def w(e, ctx, sem=it[1], holder=it[2]):
                        e.wait_ge(ctx[sem], holder[0])
                    ops[s].append(w)
                else:
                    ops[s].append(it[1])
        return ops


def _build_tables(g):
    dim = len(g)
    n = 2 ** dim
    masks = sorted(range(n), key=lambda m: (bin(m).count('1'), m))
    inv = {m: i for i, m in enumerate(masks)}
    C = np.zeros((n, n, n), dtype=np.float32)
    for i, a in enumerate(masks):
        for j, b in enumerate(masks):
            sw, at = (0, a >> 1)
            while at:
                sw += bin(at & b).count('1')
                at >>= 1
            s = -1.0 if sw & 1 else 1.0
            for t in range(dim):
                if a >> t & 1 and b >> t & 1:
                    s *= float(g[t])
            C[i, j, inv[a ^ b]] = s
    grades = np.array([bin(m).count('1') for m in masks])
    rev = np.where(grades * (grades - 1) // 2 % 2 == 1, -1.0, 1.0).astype(np.float32)
    met = np.array([float(np.prod([g[t] for t in range(dim) if m >> t & 1] + [1.0]))
                    for m in masks], dtype=np.float32)
    return C, rev, met


def _host_constants():
    CC, rev, met = _build_tables(np.array([1.0, 1.0, 1.0], dtype=np.float32))
    biv = CC[:, :, [4, 5, 6]]
    scale = 1.0 / np.sqrt(CD * NBL)
    pat = np.zeros((4, 128, 128), dtype=np.float32)
    for d in range(16):
        for c in range(3):
            pat[c, 8 * d:8 * d + 8, 8 * d:8 * d + 8] = rev[:, None] * biv[:, :, c]
        pat[3, 8 * d:8 * d + 8, 8 * d:8 * d + 8] = np.diag(rev * met * scale)
    bdp = np.tile(pat, (1, 1, NHC))
    return np.ascontiguousarray(bdp)


def build_nc():
    p = Prog()
    nc = bass.Bass("TRN2", target_bir_lowering=False)

    x_d = nc.dram_tensor("x", [T, D], F32, kind="ExternalInput").ap()
    xh_d = nc.dram_tensor("xh", [TC, D], F32, kind="ExternalInput").ap()
    wpk_d = nc.dram_tensor("wpk", [NHC, 3, 4, 128, 2048], F32, kind="ExternalInput").ap()
    bqp_d = nc.dram_tensor("bqp", [128, 16], F32, kind="ExternalInput").ap()
    bkp_d = nc.dram_tensor("bkp", [128, 16], F32, kind="ExternalInput").ap()
    bv_d = nc.dram_tensor("bv", [1, D], F32, kind="ExternalInput").ap()
    bdp_d = nc.dram_tensor("bdp", [4, 128, 1024], F32, kind="ExternalInput").ap()
    bws_d = nc.dram_tensor("bws", [3, 128, 1024], F32, kind="ExternalInput").ap()
    wrp_d = nc.dram_tensor("wrp", [128, KC * E], F32, kind="ExternalInput").ap()
    rbv_d = nc.dram_tensor("rbv", [1, E], F32, kind="ExternalInput").ap()
    ebm_d = nc.dram_tensor("ebm", [E, D], F32, kind="ExternalInput").ap()
    idm_d = nc.dram_tensor("idm", [128, 128], F32, kind="ExternalInput").ap()
    ones_d = nc.dram_tensor("onesv", [1, 128], F32, kind="ExternalInput").ap()
    msk_d = nc.dram_tensor("msk", [128, 2], F32, kind="ExternalInput").ap()
    y_d = nc.dram_tensor("y", [TC, D], F32, kind="ExternalOutput").ap()

    cin_d = [nc.dram_tensor(f"cin{g}", [T, 2 * HD], BF16, kind="Internal").ap() for g in range(4)]
    cout_d = [nc.dram_tensor(f"cout{g}", [2 * T, 2 * HD], BF16, kind="Internal").ap() for g in range(4)]

    def rdram(name, r0, r1):
        return ("d:" + name, r0, r1)

    arena = nc.sbuf_tensor("arena", [128, 209024], mybir.dt.uint8)
    arena_h = arena.__enter__()
    base = nc.lookup_mloc(arena_h).addr

    SBMAP = {}
    off = [0]

    def at(name, shape, dtype, o=None):
        if o is None:
            o = off[0]
        else:
            off[0] = o
        h = nc.alloc_sbuf_tensor_at(name, shape, dtype, offset=base + o)
        esz = mybir.dt.size(dtype)
        ncols = int(np.prod(shape[1:]))
        SBMAP[name] = (o, esz, ncols)
        off[0] = o + ((ncols * esz + 31) // 32) * 32
        return h.ap()

    def rb(name, c0=None, c1=None):
        o, esz, ncols = SBMAP[name]
        if c0 is None:
            c0, c1 = 0, ncols
        return ("sb", o + c0 * esz, o + c1 * esz)

    ln = at("ln", [128, KC * T], BF16, 0)
    wsl = [at(f"wsl{i}", [128, 2048], BF16) for i in range(6)]
    BDt = at("BD", [128, NHC * 128], BF16)
    identb = at("identb", [128, 128], BF16)
    bqps = at("bqps", [128, 16], F32)
    bkps = at("bkps", [128, 16], F32)
    sv = at("sv", [128, 40], F32)
    stats = at("stats", [128, 48], F32)
    statsB = at("statsB", [128, 48], F32)
    onesb = at("onesb", [1, 128], F32)
    mskb = at("mskb", [128, 2], F32)
    U0 = ((off[0] + 31) // 32) * 32
    xb = [at(f"xb{i}", [128, 2048], F32, U0 + i * 8192) for i in range(3)]
    lntmp = [at(f"lntmp{i}", [128, 2048], BF16, U0 + 24576 + i * 4096) for i in range(2)]
    Qb = at("Qb", [128, 2 * T], BF16, U0 + 0)
    Kb = at("Kb", [128, 2 * T], BF16, U0 + 8192)
    Vb = at("Vb", [128, 16 * HD], BF16, U0 + 16384)
    Pb = [at(f"Pb{i}", [128, T], BF16, U0 + 24576 + i * 4096) for i in range(2)]
    Ptb = at("Ptb", [128, T], BF16, U0 + 32768)
    Ohb = at("Ohb", [128, 16 * HD], BF16, U0 + 36864)
    bdsA = at("bdsA", [128, 1024], BF16, U0 + 32768)
    bdsB = at("bdsB", [128, 1024], BF16, U0 + 34816)
    bdtmp = at("bdtmp", [128, 1024], BF16, U0 + 36864)
    bvsb = at("bvsb", [1, D], F32, U0 + 0)
    r1b = [at(f"r1b{i}", [128, D], F32, U0 + 16384 + i * 16384) for i in range(2)]
    rbs = at("rbs", [1, E], F32, U0 + 49152)
    xc = [at(f"xc{i}", [128, D], F32, i * 16384) for i in range(2)]
    ab = [[at(f"ab{pr}_{i}", [128, 2048], BF16, 32768 + (pr * 4 + i) * 4096)
           for i in range(4)] for pr in range(2)]
    abP = [at(f"abP{pr}", [128, 8192], BF16, 32768 + pr * 16384) for pr in range(2)]
    off[0] = 65536
    ln2tm = [at(f"ln2tm{i}", [128, D], BF16, 65536 + i * 8192) for i in range(2)]
    ln2fm = [at(f"ln2fm{i}", [128, KC * 128], BF16, 81920 + i * 8192) for i in range(2)]
    bvB = at("bvB", [128, D], F32, 98304)
    wrb = at("wrb", [128, KC * E], BF16, 114688)
    ebb = at("ebb", [E, D], BF16, 118784)
    brB = at("brB", [128, E], F32, 126976)
    scb, exb, swb, wdf, wdm, wdb, wdnT = [], [], [], [], [], [], []
    for i in range(2):
        o = 127232 + i * 1664
        scb.append(at(f"scb{i}", [128, E], F32, o))
        exb.append(at(f"exb{i}", [128, E], F32, o + 256))
        swb.append(at(f"swb{i}", [128, E], F32, o + 512))
        wdf.append(at(f"wdf{i}", [128, E], F32, o + 768))
        wdm.append(at(f"wdm{i}", [128, E], F32, o + 1024))
        wdb.append(at(f"wdb{i}", [128, E], BF16, o + 1280))
        wdnT.append(at(f"wdnT{i}", [E, 128], BF16, o + 1408))

    MEAN, VAR, RSTD, NB_, LNV = 0, 1, 2, 3, 4
    NM0, NM1, NMF = 5, 6, 7
    SU00, SU01, SU10, SU11 = 8, 9, 10, 11
    SSUM, RS0, RS1 = 12, 13, 14
    M8, SE, RSE, NMX, WS = 15, 16, 17, 18, 19

    def svc(i):
        return sv[:, i:i + 1]

    def rsv(i):
        return rb("sv", i, i + 1)

    Tps_h = nc.psum_tensor("Tps", [128, 1024], BF16)
    PAB_h = nc.psum_tensor("PAB", [128, 1024], F32)
    Sps_h = nc.psum_tensor("Sps", [128, 2048], F32)
    PV_h = nc.psum_tensor("PV", [128, 256], F32)
    Tps = Tps_h.__enter__().ap()
    PAB = PAB_h.__enter__().ap()
    PA = PAB[:, 0:512]
    PB = PAB[:, 512:1024]
    Sps = Sps_h.__enter__().ap()
    PVp = PV_h.__enter__().ap()
    PROJ = [PA, PB]
    S3AP = [Sps[:, 0:1024], Sps[:, 1024:2048], PAB[:, 0:1024]]
    S3RG = [("ps", 6144, 10240), ("ps", 10240, 14336), ("ps", 2048, 6144)]
    def rpT(c0=0, c1=1024):
        return ("ps", 0 + c0 * 2, 0 + c1 * 2)

    def rpA(c0=0, c1=512):
        return ("ps", 2048 + c0 * 4, 2048 + c1 * 4)

    def rpB(c0=0, c1=512):
        return ("ps", 4096 + c0 * 4, 4096 + c1 * 4)

    def rpBank(g, c0=0, c1=512):
        return rpA(c0, c1) if g % 2 == 0 else rpB(c0, c1)

    def rpS(c0=0, c1=2048):
        return ("ps", 6144 + c0 * 4, 6144 + c1 * 4)

    def rpV(c0=0, c1=256):
        return ("ps", 14336 + c0 * 4, 14336 + c1 * 4)

    p.op("sync", lambda e, c: e.dma_start(out=bqps[:, :], in_=bqp_d[:, :]),
         reads=[rdram("bqp", 0, 128)], writes=[rb("bqps")])
    p.op("sync", lambda e, c: e.dma_start(out=bkps[:, :], in_=bkp_d[:, :]),
         reads=[rdram("bkp", 0, 128)], writes=[rb("bkps")])
    p.op("sync", lambda e, c: e.dma_start(out=mskb[:, :], in_=msk_d[:, :]),
         reads=[rdram("msk", 0, 128)], writes=[rb("mskb")])
    p.op("sync", lambda e, c: e.dma_start(out=onesb[:, :], in_=ones_d[:, :]),
         reads=[rdram("onesv", 0, 1)], writes=[rb("onesb")])
    p.op("pool", lambda e, c: e.dma_start(out=identb[:, :], in_=idm_d[:, :]),
         reads=[rdram("idm", 0, 128)], writes=[rb("identb")], sem="qgp")
    p.op("pool", lambda e, c: e.dma_start(out=BDt[:, :], in_=bdp_d[3]),
         reads=[rdram("bdp", 3, 4)], writes=[rb("BD")], sem="qgp")
    for cch in range(3):
        p.op("pool", lambda e, c, cch=cch: e.dma_start(out=bdsA[:, :], in_=bdp_d[cch]),
             reads=[rdram("bdp", cch, cch + 1)], writes=[rb("bdsA")], sem="qgp")
        p.op("pool", lambda e, c, cch=cch: e.dma_start(out=bdsB[:, :], in_=bws_d[cch]),
             reads=[rdram("bws", cch, cch + 1)], writes=[rb("bdsB")], sem="qgp")
        p.op("dve", lambda e, c: e.tensor_tensor(bdtmp[:, :], bdsA[:, :], bdsB[:, :], op=ALU.mult),
             reads=[rb("bdsA"), rb("bdsB")], writes=[rb("bdtmp")])
        p.op("dve", lambda e, c: e.tensor_tensor(BDt[:, :], BDt[:, :], bdtmp[:, :], op=ALU.add),
             reads=[rb("BD"), rb("bdtmp")], writes=[rb("BD")])

    for i in range(NT):
        for h in range(2):
            idx = 2 * i + h
            bufi = idx % 3
            p.op("sync",
                 lambda e, c, i=i, h=h, bufi=bufi: e.dma_start(
                     out=xb[bufi][:, :], in_=x_d[i * 128:(i + 1) * 128, h * 2048:(h + 1) * 2048]),
                 reads=[rdram("x", i * 128, (i + 1) * 128)], writes=[rb(f"xb{bufi}")])
            for cch in range(4):
                p.op("dve",
                     lambda e, c, bufi=bufi, h=h, cch=cch: e.bn_stats(
                         stats[:, (h * 4 + cch) * 6:(h * 4 + cch) * 6 + 6],
                         xb[bufi][:, cch * 512:(cch + 1) * 512]),
                     reads=[rb(f"xb{bufi}", cch * 512, (cch + 1) * 512)],
                     writes=[rb("stats", (h * 4 + cch) * 6, (h * 4 + cch) * 6 + 6)])
        p.op("dve", lambda e, c: e.bn_aggr(sv[:, MEAN:MEAN + 2], stats[:, :]),
             reads=[rb("stats")], writes=[rb("sv", MEAN, VAR + 1)])
        p.op("dve", lambda e, c: e.tensor_scalar(svc(LNV), svc(VAR), EPS, None, op0=ALU.add),
             reads=[rsv(VAR)], writes=[rsv(LNV)])
        p.op("act", lambda e, c: e.activation(svc(LNV), svc(LNV), AF.Ln),
             reads=[rsv(LNV)], writes=[rsv(LNV)])
        p.op("act", lambda e, c: e.activation(svc(RSTD), svc(LNV), AF.Exp, scale=-0.5),
             reads=[rsv(LNV)], writes=[rsv(RSTD)])
        p.op("dve", lambda e, c: e.tensor_scalar(
            svc(NB_), svc(MEAN), svc(RSTD), -1.0, op0=ALU.mult, op1=ALU.mult),
            reads=[rsv(MEAN), rsv(RSTD)], writes=[rsv(NB_)])
        for h in range(2):
            idx = 2 * i + h
            bufi = idx % 3
            par = idx % 2
            p.op("act",
                 lambda e, c, bufi=bufi, par=par: e.activation(
                     lntmp[par][:, :], xb[bufi][:, :], AF.Identity,
                     bias=svc(NB_), scale=svc(RSTD)),
                 reads=[rb(f"xb{bufi}"), rsv(NB_), rsv(RSTD)],
                 writes=[rb(f"lntmp{par}")])
            for g in range(2):
                def trgroup(e, c, par=par, g=g):
                    inst = None
                    for j in range(8):
                        inst = e.transpose(
                            Tps[:, j * 128:(j + 1) * 128],
                            lntmp[par][:, (g * 8 + j) * 128:(g * 8 + j) * 128 + 128],
                            identb[:, :])
                    return inst
                p.op("pe", trgroup,
                     reads=[rb(f"lntmp{par}", g * 1024, (g + 1) * 1024), rb("identb")],
                     writes=[rpT()])
                kc0 = h * 16 + g * 8
                p.op("act",
                     lambda e, c, kc0=kc0, i=i: e.copy(
                         ln.rearrange("p (kc t) -> p kc t", t=T)[:, kc0:kc0 + 8, i * 128:(i + 1) * 128],
                         Tps[:, 0:1024].rearrange("p (kc t) -> p kc t", t=128)),
                     reads=[rpT()], writes=[rb("ln")])

    def slot_of(w):
        return w % 6

    def load_piece(w):
        if w >= NHC * 12:
            return
        h, rem = divmod(w, 12)
        pr, m = divmod(rem, 4)
        p.op("pool",
             lambda e, c, h=h, pr=pr, m=m, w=w: e.dma_start(
                 out=wsl[slot_of(w)][:, :], in_=wpk_d[h, pr, m]),
             reads=[rdram("wpk", w, w + 1)], writes=[rb(f"wsl{slot_of(w)}")], sem="qgp")

    for w in range(6):
        load_piece(w)

    gidx = [0]
    for h in range(NHC):
        for pr in (0, 1):
            for fc in range(2):
                for tc in range(4):
                    g = gidx[0]
                    bank = PROJ[g % 2]

                    def mmgroup(e, c, h=h, pr=pr, fc=fc, tc=tc, bank=bank):
                        inst = None
                        for m in range(4):
                            ws = wsl[slot_of(h * 12 + pr * 4 + m)]
                            for l in range(8):
                                kc = m * 8 + l
                                inst = e.matmul(
                                    bank[:, :],
                                    ws[:, l * 256 + fc * 128: l * 256 + fc * 128 + 128],
                                    ln[:, kc * T + tc * 512: kc * T + tc * 512 + 512],
                                    start=(kc == 0), stop=(kc == 31))
                        return inst
                    p.op("pe", mmgroup,
                         reads=[rb(f"wsl{slot_of(h * 12 + pr * 4 + m)}") for m in range(4)] + [rb("ln")],
                         writes=[rpBank(g)])
                    dstb = Qb if pr == 0 else Kb
                    dstn = "Qb" if pr == 0 else "Kb"
                    biasv, biasn = (bqps, "bqps") if pr == 0 else (bkps, "bkps")
                    if g % 2 == 0:
                        fnc = (lambda e, c, dstb=dstb, biasv=biasv, fc=fc, tc=tc, h=h, bank=bank:
                               e.activation(
                                   dstb[:, fc * T + tc * 512: fc * T + tc * 512 + 512],
                                   bank[:, :], AF.Identity,
                                   bias=biasv[:, h * 2 + fc: h * 2 + fc + 1], scale=1.0))
                        eng = "act"
                    else:
                        fnc = (lambda e, c, dstb=dstb, biasv=biasv, fc=fc, tc=tc, h=h, bank=bank:
                               e.tensor_scalar(
                                   dstb[:, fc * T + tc * 512: fc * T + tc * 512 + 512],
                                   bank[:, :], biasv[:, h * 2 + fc: h * 2 + fc + 1], None,
                                   op0=ALU.add))
                        eng = "dve"
                    p.op(eng, fnc,
                         reads=[rpBank(g), rb(biasn, h * 2 + fc, h * 2 + fc + 1)],
                         writes=[rb(dstn, fc * T + tc * 512, fc * T + tc * 512 + 512)])
                    gidx[0] += 1
            for m in range(4):
                load_piece(h * 12 + pr * 4 + m + 6)

        for tc16 in range(16):
            g = gidx[0]
            bank = PROJ[g % 2]

            def vgroup(e, c, h=h, tc16=tc16, bank=bank):
                inst = None
                for m in range(4):
                    ws = wsl[slot_of(h * 12 + 8 + m)]
                    for l in range(8):
                        kc = m * 8 + l
                        inst = e.matmul(
                            bank[:, 0:256],
                            ln[:, kc * T + tc16 * 128: kc * T + tc16 * 128 + 128],
                            ws[:, l * 256: l * 256 + 256],
                            start=(kc == 0), stop=(kc == 31))
                return inst
            p.op("pe", vgroup,
                 reads=[rb(f"wsl{slot_of(h * 12 + 8 + m)}") for m in range(4)] + [rb("ln")],
                 writes=[rpBank(g, 0, 256)])
            if g % 2 == 0:
                fnc = (lambda e, c, tc16=tc16, bank=bank: e.copy(
                    Vb[:, tc16 * 256: tc16 * 256 + 256], bank[:, 0:256]))
                eng = "act"
            else:
                fnc = (lambda e, c, tc16=tc16, bank=bank: e.tensor_copy(
                    Vb[:, tc16 * 256: tc16 * 256 + 256], bank[:, 0:256]))
                eng = "dve"
            p.op(eng, fnc,
                 reads=[rpBank(g, 0, 256)],
                 writes=[rb("Vb", tc16 * 256, tc16 * 256 + 256)])
            gidx[0] += 1
        for m in range(4):
            load_piece(h * 12 + 8 + m + 6)

        for fc in range(2):
            for tc in range(4):
                g = gidx[0]
                bank = PROJ[g % 2]
                p.op("pe",
                     lambda e, c, h=h, fc=fc, tc=tc, bank=bank: e.matmul(
                         bank[:, :], BDt[:, h * 128:(h + 1) * 128],
                         Qb[:, fc * T + tc * 512: fc * T + tc * 512 + 512],
                         start=True, stop=True),
                     reads=[rb("BD", h * 128, (h + 1) * 128),
                            rb("Qb", fc * T + tc * 512, fc * T + tc * 512 + 512)],
                     writes=[rpBank(g)])
                if g % 2 == 0:
                    fnc = (lambda e, c, fc=fc, tc=tc, bank=bank: e.activation(
                        Qb[:, fc * T + tc * 512: fc * T + tc * 512 + 512],
                        bank[:, :], AF.Identity))
                    eng = "act"
                else:
                    fnc = (lambda e, c, fc=fc, tc=tc, bank=bank: e.tensor_copy(
                        Qb[:, fc * T + tc * 512: fc * T + tc * 512 + 512], bank[:, :]))
                    eng = "dve"
                p.op(eng, fnc,
                     reads=[rpBank(g)],
                     writes=[rb("Qb", fc * T + tc * 512, fc * T + tc * 512 + 512)])
                gidx[0] += 1

        NMP = [(NM0, NM1, NMF), (32, 33, 34)]
        for stage in range(17):
            qt = stage if stage < 16 else None
            pq = stage - 1 if stage >= 1 else None
            if qt is not None:
                par = qt % 2
                SUc = (SU00, SU01) if par == 0 else (SU10, SU11)
                nm0, nm1, nmf = NMP[par]
                RSc = RS0 if par == 0 else RS1
                for hf in range(2):
                    sl = (2 * qt + hf) % 3
                    sap, srg = S3AP[sl], S3RG[sl]

                    def smm(e, c, qt=qt, hf=hf, sap=sap):
                        inst = None
                        for n2 in range(2):
                            for fc in range(2):
                                inst = e.matmul(
                                    sap[:, n2 * 512: n2 * 512 + 512],
                                    Qb[:, fc * T + qt * 128: fc * T + qt * 128 + 128],
                                    Kb[:, fc * T + hf * 1024 + n2 * 512: fc * T + hf * 1024 + n2 * 512 + 512],
                                    start=(fc == 0), stop=(fc == 1))
                        return inst
                    p.op("pe", smm,
                         reads=[rb("Qb", qt * 128, qt * 128 + 128),
                                rb("Qb", T + qt * 128, T + qt * 128 + 128),
                                rb("Kb", hf * 1024, hf * 1024 + 1024),
                                rb("Kb", T + hf * 1024, T + hf * 1024 + 1024)],
                         writes=[srg])
            if pq is not None:
                parp = pq % 2
                RScp = RS0 if parp == 0 else RS1
                for hf in range(2):
                    def ptgroup(e, c, parp=parp, hf=hf):
                        inst = None
                        for j in range(8):
                            inst = e.transpose(
                                Tps[:, j * 128:(j + 1) * 128],
                                Pb[parp][:, hf * 1024 + j * 128: hf * 1024 + j * 128 + 128],
                                identb[:, :])
                        return inst
                    p.op("pe", ptgroup,
                         reads=[rb(f"Pb{parp}", hf * 1024, (hf + 1) * 1024), rb("identb")],
                         writes=[rpT()])
                    if hf == 0:
                        p.op("dve", lambda e, c, hf=hf: e.tensor_copy(
                            Ptb[:, hf * 1024: hf * 1024 + 1024], Tps[:, 0:1024]),
                            reads=[rpT()], writes=[rb("Ptb", hf * 1024, (hf + 1) * 1024)])
                    else:
                        p.op("act", lambda e, c, hf=hf: e.copy(
                            Ptb[:, hf * 1024: hf * 1024 + 1024], Tps[:, 0:1024]),
                            reads=[rpT()], writes=[rb("Ptb", hf * 1024, (hf + 1) * 1024)])
            if qt is not None:
                for hf in range(2):
                    sl = (2 * qt + hf) % 3
                    sap, srg = S3AP[sl], S3RG[sl]
                    nmc = nm0 if hf == 0 else nm1
                    p.op("dve", lambda e, c, sap=sap, nmc=nmc: e.tensor_reduce(
                        svc(nmc), sap[:, :], axis=AX.X, op=ALU.max, negate=True),
                        reads=[srg], writes=[rsv(nmc)])
                p.op("dve", lambda e, c, nm0=nm0, nm1=nm1, nmf=nmf: e.tensor_tensor(
                    svc(nmf), svc(nm0), svc(nm1), op=ALU.min),
                    reads=[rsv(nm0), rsv(nm1)], writes=[rsv(nmf)])
            if pq is not None:
                def pvmm(e, c):
                    inst = None
                    for kc in range(16):
                        inst = e.matmul(PVp[:, :], Ptb[:, kc * 128:(kc + 1) * 128],
                                        Vb[:, kc * 256:(kc + 1) * 256],
                                        start=(kc == 0), stop=(kc == 15))
                    return inst
                p.op("pe", pvmm, reads=[rb("Ptb"), rb("Vb")], writes=[rpV()])
                p.op("dve",
                     lambda e, c, pq=pq, RScp=RScp: e.tensor_scalar(
                         Ohb[:, pq * 256:(pq + 1) * 256], PVp[:, :], svc(RScp), None, op0=ALU.mult),
                     reads=[rpV(), rsv(RScp)],
                     writes=[rb("Ohb", pq * 256, (pq + 1) * 256)])
            if qt is not None:
                for hf in range(2):
                    sl = (2 * qt + hf) % 3
                    sap, srg = S3AP[sl], S3RG[sl]
                    su = SUc[hf]
                    p.op("act",
                         lambda e, c, par=par, hf=hf, su=su, sap=sap, nmf=nmf: e.activation(
                             Pb[par][:, hf * 1024: hf * 1024 + 1024],
                             sap[:, :],
                             AF.Exp, bias=svc(nmf), scale=1.0, accum_out=svc(su)),
                         reads=[srg, rsv(nmf)],
                         writes=[rb(f"Pb{par}", hf * 1024, (hf + 1) * 1024), rsv(su)])
                p.op("dve", lambda e, c, SUc=SUc: e.tensor_tensor(
                    svc(SSUM), svc(SUc[0]), svc(SUc[1]), op=ALU.add),
                    reads=[rsv(SUc[0]), rsv(SUc[1])], writes=[rsv(SSUM)])
                p.op("dve", lambda e, c, RSc=RSc: e.reciprocal(svc(RSc), svc(SSUM)),
                     reads=[rsv(SSUM)], writes=[rsv(RSc)])
        grp = h // 2
        p.op("sync",
             lambda e, c, h=h, grp=grp: e.dma_start(
                 out=cin_d[grp][:, (h % 2) * HD:(h % 2 + 1) * HD].rearrange("(qt p) f -> p qt f", p=128),
                 in_=Ohb[:, :].rearrange("p (qt f) -> p qt f", f=HD)),
             reads=[rb("Ohb")], writes=[rdram(f"cin{grp}", 0, T)])
        if h % 2 == 1:
            p.op("pool", lambda e, c, grp=grp: e.collective_compute(
                "AllGather", ALU.bypass,
                replica_groups=[[0, 1], [2, 3], [4, 5], [6, 7]],
                ins=[cin_d[grp]], outs=[cout_d[grp]]),
                reads=[rdram(f"cin{grp}", 0, T)],
                writes=[rdram(f"cout{grp}", 0, 2 * T)], sem="cc")

    p.op("pool", lambda e, c: e.dma_start(out=wrb[:, :], in_=wrp_d[:, :]),
         reads=[rdram("wrp", 0, 128)], writes=[rb("wrb")], sem="qgp")
    p.op("pool", lambda e, c: e.dma_start(out=ebb[:, :], in_=ebm_d[:, :]),
         reads=[rdram("ebm", 0, E)], writes=[rb("ebb")], sem="qgp")
    p.op("sync", lambda e, c: e.dma_start(out=bvsb[:, :], in_=bv_d[:, :]),
         reads=[rdram("bv", 0, 1)], writes=[rb("bvsb")])
    p.op("sync", lambda e, c: e.dma_start(out=rbs[:, :], in_=rbv_d[:, :]),
         reads=[rdram("rbv", 0, 1)], writes=[rb("rbs")])
    for j in range(8):
        bank = PROJ[j % 2]
        p.op("pe", lambda e, c, j=j, bank=bank: e.matmul(
            bank[:, :], onesb[0:1, :], bvsb[0:1, j * 512:(j + 1) * 512],
            start=True, stop=True),
            reads=[rb("onesb"), rb("bvsb", j * 512, (j + 1) * 512)],
            writes=[rpBank(j)])
        p.op("dve", lambda e, c, j=j, bank=bank: e.tensor_copy(
            bvB[:, j * 512:(j + 1) * 512], bank[:, :]),
            reads=[rpBank(j)], writes=[rb("bvB", j * 512, (j + 1) * 512)])
    p.op("pe", lambda e, c: e.matmul(PA[:, 0:64], onesb[0:1, :], rbs[0:1, :],
                                     start=True, stop=True),
         reads=[rb("onesb"), rb("rbs")], writes=[rpA(0, 64)])
    p.op("dve", lambda e, c: e.tensor_copy(brB[:, :], PA[:, 0:64]),
         reads=[rpA(0, 64)], writes=[rb("brB")])

    SVP = [{"MEAN": 0, "VAR": 1, "RSTD": 2, "NB": 3, "LNV": 4,
            "M8": 15, "SE": 16, "RSE": 17, "NMX": 18, "WS": 19},
           {"MEAN": 20, "VAR": 21, "RSTD": 22, "NB": 23, "LNV": 24,
            "M8": 25, "SE": 26, "RSE": 27, "NMX": 28, "WS": 29}]
    statsP = [stats, statsB]
    statsN = ["stats", "statsB"]
    for t in range(NTC):
        par = t % 2
        C = SVP[par]
        stt, stn = statsP[par], statsN[par]
        abhull = ("sb", 32768 + par * 16384, 32768 + (par + 1) * 16384)

        def sva(k, C=C):
            return svc(C[k])

        def rsva(k, C=C):
            return rsv(C[k])
        p.op("sync", lambda e, c, t=t, par=par: e.dma_start(
            out=xc[par][:, :], in_=xh_d[t * 128:(t + 1) * 128, :]),
            reads=[rdram("xh", t * 128, (t + 1) * 128)], writes=[rb(f"xc{par}")])
        for grp in range(4):
            p.op("pool", lambda e, c, par=par, t=t, grp=grp: e.dma_start(
                out=abP[par].rearrange("p (i f) -> p i f", f=2048)[:, :, grp * 512:(grp + 1) * 512],
                in_=cout_d[grp].rearrange("(i tt r) f -> tt r i f", i=4, tt=8, r=128)[t]),
                reads=[rdram(f"cout{grp}", 0, 2 * T)],
                writes=[("sb", 32768 + par * 16384 + grp * 1024,
                         32768 + par * 16384 + 16384)], sem="qgp")
        r1 = r1b[par]
        r1n = f"r1b{par}"
        for half in range(2):
            a0 = ab[par][2 * half + 0]
            a1 = ab[par][2 * half + 1]
            p.op("dve", lambda e, c, half=half, a0=a0, par=par, r1=r1: e.scalar_tensor_tensor(
                r1[:, half * 2048:(half + 1) * 2048], a0[:, :], mskb[:, 0:1],
                xc[par][:, half * 2048:(half + 1) * 2048], op0=ALU.mult, op1=ALU.add),
                reads=[abhull, rb("mskb", 0, 1), rb(f"xc{par}", half * 2048, (half + 1) * 2048)],
                writes=[rb(r1n, half * 2048, (half + 1) * 2048)])
            p.op("dve", lambda e, c, half=half, a1=a1, r1=r1: e.scalar_tensor_tensor(
                r1[:, half * 2048:(half + 1) * 2048], a1[:, :], mskb[:, 1:2],
                r1[:, half * 2048:(half + 1) * 2048], op0=ALU.mult, op1=ALU.add),
                reads=[abhull, rb("mskb", 1, 2), rb(r1n, half * 2048, (half + 1) * 2048)],
                writes=[rb(r1n, half * 2048, (half + 1) * 2048)])
            p.op("dve", lambda e, c, half=half, r1=r1: e.tensor_tensor(
                r1[:, half * 2048:(half + 1) * 2048],
                r1[:, half * 2048:(half + 1) * 2048],
                bvB[:, half * 2048:(half + 1) * 2048], op=ALU.add),
                reads=[rb(r1n, half * 2048, (half + 1) * 2048),
                       rb("bvB", half * 2048, (half + 1) * 2048)],
                writes=[rb(r1n, half * 2048, (half + 1) * 2048)])
        for cch in range(8):
            p.op("dve", lambda e, c, cch=cch, r1=r1, stt=stt: e.bn_stats(
                stt[:, cch * 6:cch * 6 + 6], r1[:, cch * 512:(cch + 1) * 512]),
                reads=[rb(r1n, cch * 512, (cch + 1) * 512)],
                writes=[rb(stn, cch * 6, cch * 6 + 6)])
        p.op("dve", lambda e, c, stt=stt: e.bn_aggr(sv[:, C["MEAN"]:C["MEAN"] + 2], stt[:, :]),
             reads=[rb(stn)], writes=[rb("sv", C["MEAN"], C["VAR"] + 1)])
        p.op("dve", lambda e, c: e.tensor_scalar(sva("LNV"), sva("VAR"), EPS, None, op0=ALU.add),
             reads=[rsva("VAR")], writes=[rsva("LNV")])
        p.op("act", lambda e, c: e.activation(sva("LNV"), sva("LNV"), AF.Ln),
             reads=[rsva("LNV")], writes=[rsva("LNV")])
        p.op("act", lambda e, c: e.activation(sva("RSTD"), sva("LNV"), AF.Exp, scale=-0.5),
             reads=[rsva("LNV")], writes=[rsva("RSTD")])
        p.op("dve", lambda e, c: e.tensor_scalar(sva("NB"), sva("MEAN"), -1.0, None, op0=ALU.mult),
             reads=[rsva("MEAN")], writes=[rsva("NB")])
        lt = ln2tm[par]
        ltn = f"ln2tm{par}"
        lf = ln2fm[par]
        lfn = f"ln2fm{par}"
        for half in range(2):
            p.op("act", lambda e, c, half=half, r1=r1, lt=lt: e.activation(
                lt[:, half * 2048:(half + 1) * 2048],
                r1[:, half * 2048:(half + 1) * 2048],
                AF.Identity, bias=sva("NB"), scale=sva("RSTD")),
                reads=[rb(r1n, half * 2048, (half + 1) * 2048), rsva("NB"), rsva("RSTD")],
                writes=[rb(ltn, half * 2048, (half + 1) * 2048)])
        for g in range(4):
            def tr2group(e, c, g=g, lt=lt):
                inst = None
                for j in range(8):
                    inst = e.transpose(
                        Tps[:, j * 128:(j + 1) * 128],
                        lt[:, (g * 8 + j) * 128:(g * 8 + j) * 128 + 128],
                        identb[:, :])
                return inst
            p.op("pe", tr2group,
                 reads=[rb(ltn, g * 1024, (g + 1) * 1024), rb("identb")],
                 writes=[rpT()])
            eng = "dve" if g % 2 == 0 else "act"
            fnc = (lambda e, c, g=g, lf=lf: e.tensor_copy(
                lf.rearrange("p (kc t1) -> p kc t1", t1=128)[:, g * 8:(g + 1) * 8, :],
                Tps[:, 0:1024].rearrange("p (kc t1) -> p kc t1", t1=128))) if eng == "dve" else (
                lambda e, c, g=g, lf=lf: e.copy(
                    lf.rearrange("p (kc t1) -> p kc t1", t1=128)[:, g * 8:(g + 1) * 8, :],
                    Tps[:, 0:1024].rearrange("p (kc t1) -> p kc t1", t1=128)))
            p.op(eng, fnc, reads=[rpT()], writes=[rb(lfn)])
        def rmm(e, c, lf=lf):
            inst = None
            for kc in range(KC):
                inst = e.matmul(PA[:, 0:64], lf[:, kc * 128:(kc + 1) * 128],
                                wrb[:, kc * 64:(kc + 1) * 64],
                                start=(kc == 0), stop=(kc == 31))
            return inst
        p.op("pe", rmm, reads=[rb(lfn), rb("wrb")], writes=[rpA(0, 64)])
        sc, scn = scb[par], f"scb{par}"
        ex, exn = exb[par], f"exb{par}"
        sw, swn = swb[par], f"swb{par}"
        wf, wfn = wdf[par], f"wdf{par}"
        wm, wmn = wdm[par], f"wdm{par}"
        wb_, wbn = wdb[par], f"wdb{par}"
        wt, wtn = wdnT[par], f"wdnT{par}"
        p.op("dve", lambda e, c, sc=sc: e.tensor_tensor(
            sc[:, :], PA[:, 0:64], brB[:, :], op=ALU.add),
            reads=[rpA(0, 64), rb("brB")], writes=[rb(scn)])
        p.op("dve", lambda e, c, sc=sc: e.tensor_reduce(
            sva("NMX"), sc[:, :], axis=AX.X, op=ALU.max, negate=True),
            reads=[rb(scn)], writes=[rsva("NMX")])
        p.op("act", lambda e, c, sc=sc, ex=ex: e.activation(
            ex[:, :], sc[:, :], AF.Exp, bias=sva("NMX"), scale=1.0, accum_out=sva("SE")),
            reads=[rb(scn), rsva("NMX")], writes=[rb(exn), rsva("SE")])
        p.op("dve", lambda e, c: e.reciprocal(sva("RSE"), sva("SE")),
             reads=[rsva("SE")], writes=[rsva("RSE")])
        p.op("dve", lambda e, c, sc=sc, ex=ex: e.tensor_scalar(
            sc[:, :], ex[:, :], sva("RSE"), None, op0=ALU.mult),
            reads=[rb(exn), rsva("RSE")], writes=[rb(scn)])
        p.op("dve", lambda e, c, sc=sc, sw=sw: e.tensor_copy(sw[:, :], sc[:, :]),
             reads=[rb(scn)], writes=[rb(swn)])
        for j in range(KTOP):
            p.op("dve", lambda e, c, sw=sw: e.tensor_reduce(
                sva("M8"), sw[:, :], axis=AX.X, op=ALU.max),
                reads=[rb(swn)], writes=[rsva("M8")])
            if j < KTOP - 1:
                p.op("dve", lambda e, c, sw=sw, wm=wm: e.tensor_scalar(
                    wm[:, :], sw[:, :], sva("M8"), -2.0, op0=ALU.is_ge, op1=ALU.mult),
                    reads=[rb(swn), rsva("M8")], writes=[rb(wmn)])
                p.op("dve", lambda e, c, sw=sw, wm=wm: e.tensor_tensor(
                    sw[:, :], sw[:, :], wm[:, :], op=ALU.add),
                    reads=[rb(swn), rb(wmn)], writes=[rb(swn)])
        p.op("dve", lambda e, c, sc=sc, wf=wf: e.scalar_tensor_tensor(
            wf[:, :], sc[:, :], sva("M8"), sc[:, :], op0=ALU.is_ge, op1=ALU.mult,
            accum_out=sva("WS")),
            reads=[rb(scn), rsva("M8")], writes=[rb(wfn), rsva("WS")])
        p.op("dve", lambda e, c, wf=wf, wb_=wb_: e.tensor_copy(wb_[:, :], wf[:, :]),
             reads=[rb(wfn)], writes=[rb(wbn)])
        p.op("pe", lambda e, c, wb_=wb_: e.transpose(
            Tps[0:64, 0:128], wb_[:, 0:64], identb[:, :]),
            reads=[rb(wbn), rb("identb")], writes=[rpT(0, 128)])
        p.op("act", lambda e, c, wt=wt: e.copy(wt[0:E, :], Tps[0:64, 0:128]),
             reads=[rpT(0, 128)], writes=[rb(wtn)])
        for j in range(8):
            bank = PROJ[(j + 1) % 2]
            p.op("pe", lambda e, c, j=j, bank=bank, wt=wt: e.matmul(
                bank[:, :], wt[0:E, :], ebb[0:E, j * 512:(j + 1) * 512],
                start=True, stop=True),
                reads=[rb(wtn), rb("ebb", j * 512, (j + 1) * 512)],
                writes=[rpBank(j + 1)])
            p.op("dve", lambda e, c, j=j, bank=bank, r1=r1: e.tensor_tensor(
                r1[:, j * 512:(j + 1) * 512], bank[:, :],
                r1[:, j * 512:(j + 1) * 512], op=ALU.add),
                reads=[rpBank(j + 1), rb(r1n, j * 512, (j + 1) * 512)],
                writes=[rb(r1n, j * 512, (j + 1) * 512)])
        p.op("dve", lambda e, c, r1=r1, lt=lt: e.scalar_tensor_tensor(
            r1[:, :], lt[:, :], sva("WS"), r1[:, :], op0=ALU.mult, op1=ALU.add),
            reads=[rb(ltn), rsva("WS"), rb(r1n)], writes=[rb(r1n)])
        p.op("sync", lambda e, c, t=t, par=par, r1=r1: e.dma_start(
            out=y_d[t * 128:(t + 1) * 128, :], in_=r1[:, :]),
            reads=[rb(r1n)], writes=[rdram("y", t * 128, (t + 1) * 128)])

    p.items["sync"].append(("w", "qsy", [p.cnt["qsy"]]))
    ops = p.finalize()

    with (
        nc.semaphore("s_pe") as s_pe,
        nc.semaphore("s_dve") as s_dve,
        nc.semaphore("s_act") as s_act,
        nc.semaphore("s_pool") as s_pool,
        nc.semaphore("s_qsy") as s_qsy,
        nc.semaphore("s_qsc") as s_qsc,
        nc.semaphore("s_qgp") as s_qgp,
        nc.semaphore("s_cc") as s_cc,
        nc.Block() as block,
    ):
        ctx = {"pe": s_pe, "dve": s_dve, "act": s_act, "pool": s_pool,
               "qsy": s_qsy, "qsc": s_qsc, "qgp": s_qgp, "cc": s_cc}

        @block.tensor
        def _(e):
            for f in ops["pe"]:
                f(e, ctx)

        @block.vector
        def _(e):
            for f in ops["dve"]:
                f(e, ctx)

        @block.scalar
        def _(e):
            for f in ops["act"]:
                f(e, ctx)

        @block.gpsimd
        def _(e):
            for f in ops["pool"]:
                f(e, ctx)

        @block.sync
        def _(e):
            for f in ops["sync"]:
                f(e, ctx)

    return nc


_NC_CACHE = None


def _get_nc():
    global _NC_CACHE
    if _NC_CACHE is None:
        _NC_CACHE = build_nc()
    return _NC_CACHE


def kernel(L0_attn_x, L0_attn_Wq, L0_attn_bq, L0_attn_Wk, L0_attn_bk, L0_attn_Wv, L0_attn_bv,
           L0_attn_bias_weight, L0_ffn_router_weight, L0_ffn_router_bias, L0_ffn_expert_bias):
    x = np.asarray(L0_attn_x, dtype=np.float32)
    Wq = np.asarray(L0_attn_Wq, dtype=np.float32)
    Wk = np.asarray(L0_attn_Wk, dtype=np.float32)
    Wv = np.asarray(L0_attn_Wv, dtype=np.float32)
    bq = np.asarray(L0_attn_bq, dtype=np.float32)
    bk = np.asarray(L0_attn_bk, dtype=np.float32)
    bv = np.asarray(L0_attn_bv, dtype=np.float32)
    bw = np.asarray(L0_attn_bias_weight, dtype=np.float32)
    Wr = np.asarray(L0_ffn_router_weight, dtype=np.float32)
    rb_ = np.asarray(L0_ffn_router_bias, dtype=np.float32)
    eb = np.asarray(L0_ffn_expert_bias, dtype=np.float32)

    bdp = _host_constants()
    ident = np.eye(128, dtype=np.float32)
    onesv = np.ones((1, 128), dtype=np.float32)
    wrp = np.ascontiguousarray(Wr.T.reshape(KC, 128, E).transpose(1, 0, 2).reshape(128, KC * E))

    def pack_w(W, s):
        WT = W.T[:, s * 2048:(s + 1) * 2048]
        out = np.empty((NHC, 4, 128, 2048), dtype=np.float32)
        for j in range(NHC):
            sl = WT[:, j * HD:(j + 1) * HD]
            a = sl.reshape(4, 8, 128, HD).transpose(0, 2, 1, 3)
            out[j] = a.reshape(4, 128, 8 * HD)
        return out

    in_maps = []
    for core in range(8):
        b, s = divmod(core, 2)
        wpk = np.stack([pack_w(Wq, s), pack_w(Wk, s), pack_w(Wv, s)], axis=1)
        bqs = bq[s * 2048:(s + 1) * 2048].reshape(NHC, 2, 128).transpose(2, 0, 1).reshape(128, 16)
        bks = bk[s * 2048:(s + 1) * 2048].reshape(NHC, 2, 128).transpose(2, 0, 1).reshape(128, 16)
        bws = np.broadcast_to(
            bw[s * NHC:(s + 1) * NHC].T.reshape(3, 1, NHC, 1), (3, 128, NHC, 128)
        ).reshape(3, 128, NHC * 128)
        msk = np.zeros((128, 2), dtype=np.float32)
        msk[:, s] = 1.0
        in_maps.append({
            "x": np.ascontiguousarray(x[b]),
            "xh": np.ascontiguousarray(x[b, s * TC:(s + 1) * TC]),
            "wpk": np.ascontiguousarray(wpk),
            "bqp": np.ascontiguousarray(bqs),
            "bkp": np.ascontiguousarray(bks),
            "bv": bv.reshape(1, D),
            "bdp": bdp,
            "bws": np.ascontiguousarray(bws),
            "wrp": wrp,
            "rbv": rb_.reshape(1, E),
            "ebm": eb,
            "idm": ident,
            "onesv": onesv,
            "msk": msk,
        })

    nc = _get_nc()
    res = run_bass_kernel_spmd(nc, in_maps, core_ids=list(range(8)))
    out = np.empty((B, L, D), dtype=np.float32)
    for core in range(8):
        b, s = divmod(core, 2)
        out[b, s * TC:(s + 1) * TC, :] = res.results[core]["y"]
    return out


# version 19
# speedup vs baseline: 1.2495x; 1.0308x over previous
import numpy as np
import concourse.bass as bass
import concourse.mybir as mybir
from concourse.bass_utils import run_bass_kernel_spmd

F32 = mybir.dt.float32
BF16 = mybir.dt.bfloat16
AF = mybir.ActivationFunctionType
ALU = mybir.AluOpType
AX = mybir.AxisListType

B, L, D = 4, 2048, 4096
NH, CD, NBL = 16, 32, 8
HD = CD * NBL
NHC = NH // 2
T = L
TC = L // 2
KC = D // 128
NT = T // 128
NTC = TC // 128
EPS = 1e-5
E = 64
KTOP = 8

STREAMS = ("pe", "dve", "act", "pool", "sync")
DEFSEM = {"pe": "pe", "dve": "dve", "act": "act", "pool": "pool", "sync": "qsy"}
QSEMS = ("qsy", "qsc", "qgp", "cc")
ALLSEMS = ("pe", "dve", "act", "pool", "qsy", "qsc", "qgp", "cc")


class Prog:

    def __init__(self):
        self.items = {s: [] for s in STREAMS}
        self.cnt = {s: 0 for s in ALLSEMS}
        self.lastwait = {}
        self.wr = {}
        self.rd = {}
        self.nwaits = 0

    def _wait(self, stream, sem, val):
        key = (stream, sem)
        if self.lastwait.get(key, -1) >= val:
            return
        self.lastwait[key] = val
        self.nwaits += 1
        self.items[stream].append(("w", sem, [val]))

    def op(self, stream, fn, reads=(), writes=(), sem=None):
        semname = sem or DEFSEM[stream]
        amt = 1 if semname == "cc" else (16 if semname in QSEMS else 1)
        need = {}
        for sp, lo, hi in reads:
            for l2, h2, ev in self.wr.get(sp, ()):
                if l2 < hi and lo < h2:
                    k = ev[0]
                    if ev[1] > need.get(k, -1):
                        need[k] = ev[1]
        for sp, lo, hi in writes:
            for l2, h2, ev in self.wr.get(sp, ()):
                if l2 < hi and lo < h2:
                    k = ev[0]
                    if ev[1] > need.get(k, -1):
                        need[k] = ev[1]
            for (l2, h2, s2), v2 in self.rd.get(sp, {}).items():
                if l2 < hi and lo < h2:
                    if v2 > need.get(s2, -1):
                        need[s2] = v2
        for s2 in sorted(need):
            if stream == "pe" and s2 == "pe":
                continue
            self._wait(stream, s2, need[s2])
        self.cnt[semname] += amt
        ev = (semname, self.cnt[semname])

        if semname == "cc":
            def run(e, ctx, fn=fn, s=semname):
                fn(e, ctx).then_inc(ctx[s])
        else:
            def run(e, ctx, fn=fn, s=semname, a=amt):
                fn(e, ctx).then_inc(ctx[s], a)
        self.items[stream].append(("op", run, semname if semname in ("qsy", "qsc", "qgp") else None))
        for sp, lo, hi in writes:
            wl = self.wr.setdefault(sp, [])
            wl[:] = [w for w in wl if not (w[0] >= lo and w[1] <= hi)]
            wl.append((lo, hi, ev))
            rl = self.rd.get(sp)
            if rl:
                for k in [k for k in rl if k[0] >= lo and k[1] <= hi]:
                    del rl[k]
        for sp, lo, hi in reads:
            rl = self.rd.setdefault(sp, {})
            k = (lo, hi, ev[0])
            if ev[1] > rl.get(k, -1):
                rl[k] = ev[1]
        return ev

    def finalize(self):
        wanted = {q: set() for q in ("qsy", "qsc", "qgp")}
        for s in STREAMS:
            for it in self.items[s]:
                if it[0] == "w" and it[1] in wanted:
                    wanted[it[1]].add(it[2][0])
        for q, s in (("qsy", "sync"), ("qsc", "act"), ("qgp", "pool")):
            out = []
            cum = 0
            for it in self.items[s]:
                out.append(it)
                if it[0] == "op" and it[2] == q:
                    cum += 16
                    if cum in wanted[q]:
                        out.append(("w", q, [cum]))
            self.items[s] = out
        ops = {s: [] for s in STREAMS}
        for s in STREAMS:
            for it in self.items[s]:
                if it[0] == "w":
                    def w(e, ctx, sem=it[1], holder=it[2]):
                        e.wait_ge(ctx[sem], holder[0])
                    ops[s].append(w)
                else:
                    ops[s].append(it[1])
        return ops


def _build_tables(g):
    dim = len(g)
    n = 2 ** dim
    masks = sorted(range(n), key=lambda m: (bin(m).count('1'), m))
    inv = {m: i for i, m in enumerate(masks)}
    C = np.zeros((n, n, n), dtype=np.float32)
    for i, a in enumerate(masks):
        for j, b in enumerate(masks):
            sw, at = (0, a >> 1)
            while at:
                sw += bin(at & b).count('1')
                at >>= 1
            s = -1.0 if sw & 1 else 1.0
            for t in range(dim):
                if a >> t & 1 and b >> t & 1:
                    s *= float(g[t])
            C[i, j, inv[a ^ b]] = s
    grades = np.array([bin(m).count('1') for m in masks])
    rev = np.where(grades * (grades - 1) // 2 % 2 == 1, -1.0, 1.0).astype(np.float32)
    met = np.array([float(np.prod([g[t] for t in range(dim) if m >> t & 1] + [1.0]))
                    for m in masks], dtype=np.float32)
    return C, rev, met


def _host_constants():
    CC, rev, met = _build_tables(np.array([1.0, 1.0, 1.0], dtype=np.float32))
    biv = CC[:, :, [4, 5, 6]]
    scale = 1.0 / np.sqrt(CD * NBL)
    pat = np.zeros((4, 128, 128), dtype=np.float32)
    for d in range(16):
        for c in range(3):
            pat[c, 8 * d:8 * d + 8, 8 * d:8 * d + 8] = rev[:, None] * biv[:, :, c]
        pat[3, 8 * d:8 * d + 8, 8 * d:8 * d + 8] = np.diag(rev * met * scale)
    bdp = np.tile(pat, (1, 1, NHC))
    return np.ascontiguousarray(bdp)


def build_nc():
    p = Prog()
    nc = bass.Bass("TRN2", target_bir_lowering=False)

    x_d = nc.dram_tensor("x", [T, D], F32, kind="ExternalInput").ap()
    xh_d = nc.dram_tensor("xh", [TC, D], F32, kind="ExternalInput").ap()
    wpk_d = nc.dram_tensor("wpk", [NHC, 3, 4, 128, 2048], F32, kind="ExternalInput").ap()
    bqp_d = nc.dram_tensor("bqp", [128, 16], F32, kind="ExternalInput").ap()
    bkp_d = nc.dram_tensor("bkp", [128, 16], F32, kind="ExternalInput").ap()
    bv_d = nc.dram_tensor("bv", [1, D], F32, kind="ExternalInput").ap()
    bdp_d = nc.dram_tensor("bdp", [4, 128, 1024], F32, kind="ExternalInput").ap()
    bws_d = nc.dram_tensor("bws", [3, 128, 1024], F32, kind="ExternalInput").ap()
    wrp_d = nc.dram_tensor("wrp", [128, KC * E], F32, kind="ExternalInput").ap()
    rbv_d = nc.dram_tensor("rbv", [1, E], F32, kind="ExternalInput").ap()
    ebm_d = nc.dram_tensor("ebm", [E, D], F32, kind="ExternalInput").ap()
    idm_d = nc.dram_tensor("idm", [128, 128], F32, kind="ExternalInput").ap()
    ones_d = nc.dram_tensor("onesv", [1, 128], F32, kind="ExternalInput").ap()
    msk_d = nc.dram_tensor("msk", [128, 2], F32, kind="ExternalInput").ap()
    y_d = nc.dram_tensor("y", [TC, D], F32, kind="ExternalOutput").ap()

    cin_d = [nc.dram_tensor(f"cin{g}", [T, 2 * HD], BF16, kind="Internal").ap() for g in range(4)]
    cout_d = [nc.dram_tensor(f"cout{g}", [2 * T, 2 * HD], BF16, kind="Internal").ap() for g in range(4)]
    dcin_d = nc.dram_tensor("dcin", [128, 8], BF16, kind="Internal").ap()
    dcout_d = nc.dram_tensor("dcout", [256, 8], BF16, kind="Internal").ap()

    def rdram(name, r0, r1):
        return ("d:" + name, r0, r1)

    arena = nc.sbuf_tensor("arena", [128, 209024], mybir.dt.uint8)
    arena_h = arena.__enter__()
    base = nc.lookup_mloc(arena_h).addr

    SBMAP = {}
    off = [0]

    def at(name, shape, dtype, o=None):
        if o is None:
            o = off[0]
        else:
            off[0] = o
        h = nc.alloc_sbuf_tensor_at(name, shape, dtype, offset=base + o)
        esz = mybir.dt.size(dtype)
        ncols = int(np.prod(shape[1:]))
        SBMAP[name] = (o, esz, ncols)
        off[0] = o + ((ncols * esz + 31) // 32) * 32
        return h.ap()

    def rb(name, c0=None, c1=None):
        o, esz, ncols = SBMAP[name]
        if c0 is None:
            c0, c1 = 0, ncols
        return ("sb", o + c0 * esz, o + c1 * esz)

    ln = at("ln", [128, KC * T], BF16, 0)
    wsl = [at(f"wsl{i}", [128, 2048], BF16) for i in range(6)]
    BDt = at("BD", [128, NHC * 128], BF16)
    identb = at("identb", [128, 128], BF16)
    bqps = at("bqps", [128, 16], F32)
    bkps = at("bkps", [128, 16], F32)
    sv = at("sv", [128, 40], F32)
    stats = at("stats", [128, 48], F32)
    statsB = at("statsB", [128, 48], F32)
    onesb = at("onesb", [1, 128], F32)
    mskb = at("mskb", [128, 2], F32)
    U0 = ((off[0] + 31) // 32) * 32
    xb = [at(f"xb{i}", [128, 2048], F32, U0 + i * 8192) for i in range(3)]
    lntmp = [at(f"lntmp{i}", [128, 2048], BF16, U0 + 24576 + i * 4096) for i in range(2)]
    Qb = at("Qb", [128, 2 * T], BF16, U0 + 0)
    Kb = at("Kb", [128, 2 * T], BF16, U0 + 8192)
    Vb = at("Vb", [128, 16 * HD], BF16, U0 + 16384)
    Pb = [at(f"Pb{i}", [128, T], BF16, U0 + 24576 + i * 4096) for i in range(2)]
    Ptb = at("Ptb", [128, T], BF16, U0 + 32768)
    Ohb = at("Ohb", [128, 16 * HD], BF16, U0 + 36864)
    bdsA = at("bdsA", [128, 1024], BF16, U0 + 32768)
    bdsB = at("bdsB", [128, 1024], BF16, U0 + 34816)
    bdtmp = at("bdtmp", [128, 1024], BF16, U0 + 36864)
    bvsb = at("bvsb", [1, D], F32, U0 + 0)
    r1b = [at(f"r1b{i}", [128, D], F32, U0 + 16384 + i * 16384) for i in range(2)]
    rbs = at("rbs", [1, E], F32, U0 + 49152)
    xc = [at(f"xc{i}", [128, D], F32, i * 16384) for i in range(2)]
    ab = [[at(f"ab{pr}_{i}", [128, 2048], BF16, 32768 + (pr * 4 + i) * 4096)
           for i in range(4)] for pr in range(2)]
    abP = [at(f"abP{pr}", [128, 8192], BF16, 32768 + pr * 16384) for pr in range(2)]
    off[0] = 65536
    ln2tm = [at(f"ln2tm{i}", [128, D], BF16, 65536 + i * 8192) for i in range(2)]
    ln2fm = [at(f"ln2fm{i}", [128, KC * 128], BF16, 81920 + i * 8192) for i in range(2)]
    bvB = at("bvB", [128, D], F32, 98304)
    wrb = at("wrb", [128, KC * E], BF16, 114688)
    ebb = at("ebb", [E, D], BF16, 118784)
    brB = at("brB", [128, E], F32, 126976)
    scb, exb, swb, wdf, wdm, wdb, wdnT = [], [], [], [], [], [], []
    for i in range(2):
        o = 127232 + i * 1664
        scb.append(at(f"scb{i}", [128, E], F32, o))
        exb.append(at(f"exb{i}", [128, E], F32, o + 256))
        swb.append(at(f"swb{i}", [128, E], F32, o + 512))
        wdf.append(at(f"wdf{i}", [128, E], F32, o + 768))
        wdm.append(at(f"wdm{i}", [128, E], F32, o + 1024))
        wdb.append(at(f"wdb{i}", [128, E], BF16, o + 1280))
        wdnT.append(at(f"wdnT{i}", [E, 128], BF16, o + 1408))

    MEAN, VAR, RSTD, NB_, LNV = 0, 1, 2, 3, 4
    NM0, NM1, NMF = 5, 6, 7
    SU00, SU01, SU10, SU11 = 8, 9, 10, 11
    SSUM, RS0, RS1 = 12, 13, 14
    M8, SE, RSE, NMX, WS = 15, 16, 17, 18, 19

    def svc(i):
        return sv[:, i:i + 1]

    def rsv(i):
        return rb("sv", i, i + 1)

    Tps_h = nc.psum_tensor("Tps", [128, 1024], BF16)
    PAB_h = nc.psum_tensor("PAB", [128, 1024], F32)
    Sps_h = nc.psum_tensor("Sps", [128, 2048], F32)
    PV_h = nc.psum_tensor("PV", [128, 256], F32)
    Tps = Tps_h.__enter__().ap()
    PAB = PAB_h.__enter__().ap()
    PA = PAB[:, 0:512]
    PB = PAB[:, 512:1024]
    Sps = Sps_h.__enter__().ap()
    PVp = PV_h.__enter__().ap()
    PROJ = [PA, PB]
    S3AP = [Sps[:, 0:1024], Sps[:, 1024:2048], PAB[:, 0:1024]]
    S3RG = [("ps", 6144, 10240), ("ps", 10240, 14336), ("ps", 2048, 6144)]
    def rpT(c0=0, c1=1024):
        return ("ps", 0 + c0 * 2, 0 + c1 * 2)

    def rpA(c0=0, c1=512):
        return ("ps", 2048 + c0 * 4, 2048 + c1 * 4)

    def rpB(c0=0, c1=512):
        return ("ps", 4096 + c0 * 4, 4096 + c1 * 4)

    def rpBank(g, c0=0, c1=512):
        return rpA(c0, c1) if g % 2 == 0 else rpB(c0, c1)

    def rpS(c0=0, c1=2048):
        return ("ps", 6144 + c0 * 4, 6144 + c1 * 4)

    def rpV(c0=0, c1=256):
        return ("ps", 14336 + c0 * 4, 14336 + c1 * 4)

    p.op("sync", lambda e, c: e.dma_start(out=bqps[:, :], in_=bqp_d[:, :]),
         reads=[rdram("bqp", 0, 128)], writes=[rb("bqps")])
    p.op("sync", lambda e, c: e.dma_start(out=bkps[:, :], in_=bkp_d[:, :]),
         reads=[rdram("bkp", 0, 128)], writes=[rb("bkps")])
    p.op("sync", lambda e, c: e.dma_start(out=mskb[:, :], in_=msk_d[:, :]),
         reads=[rdram("msk", 0, 128)], writes=[rb("mskb")])
    p.op("sync", lambda e, c: e.dma_start(out=onesb[:, :], in_=ones_d[:, :]),
         reads=[rdram("onesv", 0, 1)], writes=[rb("onesb")])
    p.op("pool", lambda e, c: e.dma_start(out=identb[:, :], in_=idm_d[:, :]),
         reads=[rdram("idm", 0, 128)], writes=[rb("identb")], sem="qgp")
    p.op("pool", lambda e, c: e.dma_start(out=BDt[:, :], in_=bdp_d[3]),
         reads=[rdram("bdp", 3, 4)], writes=[rb("BD")], sem="qgp")
    for cch in range(3):
        p.op("pool", lambda e, c, cch=cch: e.dma_start(out=bdsA[:, :], in_=bdp_d[cch]),
             reads=[rdram("bdp", cch, cch + 1)], writes=[rb("bdsA")], sem="qgp")
        p.op("pool", lambda e, c, cch=cch: e.dma_start(out=bdsB[:, :], in_=bws_d[cch]),
             reads=[rdram("bws", cch, cch + 1)], writes=[rb("bdsB")], sem="qgp")
        p.op("dve", lambda e, c: e.tensor_tensor(bdtmp[:, :], bdsA[:, :], bdsB[:, :], op=ALU.mult),
             reads=[rb("bdsA"), rb("bdsB")], writes=[rb("bdtmp")])
        p.op("dve", lambda e, c: e.tensor_tensor(BDt[:, :], BDt[:, :], bdtmp[:, :], op=ALU.add),
             reads=[rb("BD"), rb("bdtmp")], writes=[rb("BD")])

    for i in range(NT):
        for h in range(2):
            idx = 2 * i + h
            bufi = idx % 3
            p.op("sync",
                 lambda e, c, i=i, h=h, bufi=bufi: e.dma_start(
                     out=xb[bufi][:, :], in_=x_d[i * 128:(i + 1) * 128, h * 2048:(h + 1) * 2048]),
                 reads=[rdram("x", i * 128, (i + 1) * 128)], writes=[rb(f"xb{bufi}")])
            for cch in range(4):
                p.op("dve",
                     lambda e, c, bufi=bufi, h=h, cch=cch: e.bn_stats(
                         stats[:, (h * 4 + cch) * 6:(h * 4 + cch) * 6 + 6],
                         xb[bufi][:, cch * 512:(cch + 1) * 512]),
                     reads=[rb(f"xb{bufi}", cch * 512, (cch + 1) * 512)],
                     writes=[rb("stats", (h * 4 + cch) * 6, (h * 4 + cch) * 6 + 6)])
        p.op("dve", lambda e, c: e.bn_aggr(sv[:, MEAN:MEAN + 2], stats[:, :]),
             reads=[rb("stats")], writes=[rb("sv", MEAN, VAR + 1)])
        p.op("dve", lambda e, c: e.tensor_scalar(svc(LNV), svc(VAR), EPS, None, op0=ALU.add),
             reads=[rsv(VAR)], writes=[rsv(LNV)])
        p.op("act", lambda e, c: e.activation(svc(LNV), svc(LNV), AF.Ln),
             reads=[rsv(LNV)], writes=[rsv(LNV)])
        p.op("act", lambda e, c: e.activation(svc(RSTD), svc(LNV), AF.Exp, scale=-0.5),
             reads=[rsv(LNV)], writes=[rsv(RSTD)])
        p.op("dve", lambda e, c: e.tensor_scalar(
            svc(NB_), svc(MEAN), svc(RSTD), -1.0, op0=ALU.mult, op1=ALU.mult),
            reads=[rsv(MEAN), rsv(RSTD)], writes=[rsv(NB_)])
        for h in range(2):
            idx = 2 * i + h
            bufi = idx % 3
            par = idx % 2
            p.op("act",
                 lambda e, c, bufi=bufi, par=par: e.activation(
                     lntmp[par][:, :], xb[bufi][:, :], AF.Identity,
                     bias=svc(NB_), scale=svc(RSTD)),
                 reads=[rb(f"xb{bufi}"), rsv(NB_), rsv(RSTD)],
                 writes=[rb(f"lntmp{par}")])
            for g in range(2):
                def trgroup(e, c, par=par, g=g):
                    inst = None
                    for j in range(8):
                        inst = e.transpose(
                            Tps[:, j * 128:(j + 1) * 128],
                            lntmp[par][:, (g * 8 + j) * 128:(g * 8 + j) * 128 + 128],
                            identb[:, :])
                    return inst
                p.op("pe", trgroup,
                     reads=[rb(f"lntmp{par}", g * 1024, (g + 1) * 1024), rb("identb")],
                     writes=[rpT()])
                kc0 = h * 16 + g * 8
                p.op("act",
                     lambda e, c, kc0=kc0, i=i: e.copy(
                         ln.rearrange("p (kc t) -> p kc t", t=T)[:, kc0:kc0 + 8, i * 128:(i + 1) * 128],
                         Tps[:, 0:1024].rearrange("p (kc t) -> p kc t", t=128)),
                     reads=[rpT()], writes=[rb("ln")])

    def slot_of(w):
        return w % 6

    def load_piece(w):
        if w >= NHC * 12:
            return
        h, rem = divmod(w, 12)
        pr, m = divmod(rem, 4)
        p.op("pool",
             lambda e, c, h=h, pr=pr, m=m, w=w: e.dma_start(
                 out=wsl[slot_of(w)][:, :], in_=wpk_d[h, pr, m]),
             reads=[rdram("wpk", w, w + 1)], writes=[rb(f"wsl{slot_of(w)}")], sem="qgp")

    for w in range(6):
        load_piece(w)

    gidx = [0]
    for h in range(NHC):
        for pr in (0, 1):
            for fc in range(2):
                for tc in range(4):
                    g = gidx[0]
                    bank = PROJ[g % 2]

                    def mmgroup(e, c, h=h, pr=pr, fc=fc, tc=tc, bank=bank):
                        inst = None
                        for m in range(4):
                            ws = wsl[slot_of(h * 12 + pr * 4 + m)]
                            for l in range(8):
                                kc = m * 8 + l
                                inst = e.matmul(
                                    bank[:, :],
                                    ws[:, l * 256 + fc * 128: l * 256 + fc * 128 + 128],
                                    ln[:, kc * T + tc * 512: kc * T + tc * 512 + 512],
                                    start=(kc == 0), stop=(kc == 31))
                        return inst
                    p.op("pe", mmgroup,
                         reads=[rb(f"wsl{slot_of(h * 12 + pr * 4 + m)}") for m in range(4)] + [rb("ln")],
                         writes=[rpBank(g)])
                    dstb = Qb if pr == 0 else Kb
                    dstn = "Qb" if pr == 0 else "Kb"
                    biasv, biasn = (bqps, "bqps") if pr == 0 else (bkps, "bkps")
                    if g % 2 == 0:
                        fnc = (lambda e, c, dstb=dstb, biasv=biasv, fc=fc, tc=tc, h=h, bank=bank:
                               e.activation(
                                   dstb[:, fc * T + tc * 512: fc * T + tc * 512 + 512],
                                   bank[:, :], AF.Identity,
                                   bias=biasv[:, h * 2 + fc: h * 2 + fc + 1], scale=1.0))
                        eng = "act"
                    else:
                        fnc = (lambda e, c, dstb=dstb, biasv=biasv, fc=fc, tc=tc, h=h, bank=bank:
                               e.tensor_scalar(
                                   dstb[:, fc * T + tc * 512: fc * T + tc * 512 + 512],
                                   bank[:, :], biasv[:, h * 2 + fc: h * 2 + fc + 1], None,
                                   op0=ALU.add))
                        eng = "dve"
                    p.op(eng, fnc,
                         reads=[rpBank(g), rb(biasn, h * 2 + fc, h * 2 + fc + 1)],
                         writes=[rb(dstn, fc * T + tc * 512, fc * T + tc * 512 + 512)])
                    gidx[0] += 1
            for m in range(4):
                load_piece(h * 12 + pr * 4 + m + 6)

        for tc16 in range(16):
            g = gidx[0]
            bank = PROJ[g % 2]

            def vgroup(e, c, h=h, tc16=tc16, bank=bank):
                inst = None
                for m in range(4):
                    ws = wsl[slot_of(h * 12 + 8 + m)]
                    for l in range(8):
                        kc = m * 8 + l
                        inst = e.matmul(
                            bank[:, 0:256],
                            ln[:, kc * T + tc16 * 128: kc * T + tc16 * 128 + 128],
                            ws[:, l * 256: l * 256 + 256],
                            start=(kc == 0), stop=(kc == 31))
                return inst
            p.op("pe", vgroup,
                 reads=[rb(f"wsl{slot_of(h * 12 + 8 + m)}") for m in range(4)] + [rb("ln")],
                 writes=[rpBank(g, 0, 256)])
            if g % 2 == 0:
                fnc = (lambda e, c, tc16=tc16, bank=bank: e.copy(
                    Vb[:, tc16 * 256: tc16 * 256 + 256], bank[:, 0:256]))
                eng = "act"
            else:
                fnc = (lambda e, c, tc16=tc16, bank=bank: e.tensor_copy(
                    Vb[:, tc16 * 256: tc16 * 256 + 256], bank[:, 0:256]))
                eng = "dve"
            p.op(eng, fnc,
                 reads=[rpBank(g, 0, 256)],
                 writes=[rb("Vb", tc16 * 256, tc16 * 256 + 256)])
            gidx[0] += 1
        for m in range(4):
            load_piece(h * 12 + 8 + m + 6)

        for fc in range(2):
            for tc in range(4):
                g = gidx[0]
                bank = PROJ[g % 2]
                p.op("pe",
                     lambda e, c, h=h, fc=fc, tc=tc, bank=bank: e.matmul(
                         bank[:, :], BDt[:, h * 128:(h + 1) * 128],
                         Qb[:, fc * T + tc * 512: fc * T + tc * 512 + 512],
                         start=True, stop=True),
                     reads=[rb("BD", h * 128, (h + 1) * 128),
                            rb("Qb", fc * T + tc * 512, fc * T + tc * 512 + 512)],
                     writes=[rpBank(g)])
                if g % 2 == 0:
                    fnc = (lambda e, c, fc=fc, tc=tc, bank=bank: e.activation(
                        Qb[:, fc * T + tc * 512: fc * T + tc * 512 + 512],
                        bank[:, :], AF.Identity))
                    eng = "act"
                else:
                    fnc = (lambda e, c, fc=fc, tc=tc, bank=bank: e.tensor_copy(
                        Qb[:, fc * T + tc * 512: fc * T + tc * 512 + 512], bank[:, :]))
                    eng = "dve"
                p.op(eng, fnc,
                     reads=[rpBank(g)],
                     writes=[rb("Qb", fc * T + tc * 512, fc * T + tc * 512 + 512)])
                gidx[0] += 1

        NMP = [(NM0, NM1, NMF), (32, 33, 34)]
        for stage in range(17):
            qt = stage if stage < 16 else None
            pq = stage - 1 if stage >= 1 else None
            if qt is not None:
                par = qt % 2
                SUc = (SU00, SU01) if par == 0 else (SU10, SU11)
                nm0, nm1, nmf = NMP[par]
                RSc = RS0 if par == 0 else RS1
                for hf in range(2):
                    sl = (2 * qt + hf) % 3
                    sap, srg = S3AP[sl], S3RG[sl]

                    def smm(e, c, qt=qt, hf=hf, sap=sap):
                        inst = None
                        for n2 in range(2):
                            for fc in range(2):
                                inst = e.matmul(
                                    sap[:, n2 * 512: n2 * 512 + 512],
                                    Qb[:, fc * T + qt * 128: fc * T + qt * 128 + 128],
                                    Kb[:, fc * T + hf * 1024 + n2 * 512: fc * T + hf * 1024 + n2 * 512 + 512],
                                    start=(fc == 0), stop=(fc == 1))
                        return inst
                    p.op("pe", smm,
                         reads=[rb("Qb", qt * 128, qt * 128 + 128),
                                rb("Qb", T + qt * 128, T + qt * 128 + 128),
                                rb("Kb", hf * 1024, hf * 1024 + 1024),
                                rb("Kb", T + hf * 1024, T + hf * 1024 + 1024)],
                         writes=[srg])
            if pq is not None:
                parp = pq % 2
                RScp = RS0 if parp == 0 else RS1
                for hf in range(2):
                    def ptgroup(e, c, parp=parp, hf=hf):
                        inst = None
                        for j in range(8):
                            inst = e.transpose(
                                Tps[:, j * 128:(j + 1) * 128],
                                Pb[parp][:, hf * 1024 + j * 128: hf * 1024 + j * 128 + 128],
                                identb[:, :])
                        return inst
                    p.op("pe", ptgroup,
                         reads=[rb(f"Pb{parp}", hf * 1024, (hf + 1) * 1024), rb("identb")],
                         writes=[rpT()])
                    if hf == 0:
                        p.op("dve", lambda e, c, hf=hf: e.tensor_copy(
                            Ptb[:, hf * 1024: hf * 1024 + 1024], Tps[:, 0:1024]),
                            reads=[rpT()], writes=[rb("Ptb", hf * 1024, (hf + 1) * 1024)])
                    else:
                        p.op("act", lambda e, c, hf=hf: e.copy(
                            Ptb[:, hf * 1024: hf * 1024 + 1024], Tps[:, 0:1024]),
                            reads=[rpT()], writes=[rb("Ptb", hf * 1024, (hf + 1) * 1024)])
            if qt is not None:
                for hf in range(2):
                    sl = (2 * qt + hf) % 3
                    sap, srg = S3AP[sl], S3RG[sl]
                    nmc = nm0 if hf == 0 else nm1
                    p.op("dve", lambda e, c, sap=sap, nmc=nmc: e.tensor_reduce(
                        svc(nmc), sap[:, :], axis=AX.X, op=ALU.max, negate=True),
                        reads=[srg], writes=[rsv(nmc)])
                p.op("dve", lambda e, c, nm0=nm0, nm1=nm1, nmf=nmf: e.tensor_tensor(
                    svc(nmf), svc(nm0), svc(nm1), op=ALU.min),
                    reads=[rsv(nm0), rsv(nm1)], writes=[rsv(nmf)])
            if pq is not None:
                def pvmm(e, c):
                    inst = None
                    for kc in range(16):
                        inst = e.matmul(PVp[:, :], Ptb[:, kc * 128:(kc + 1) * 128],
                                        Vb[:, kc * 256:(kc + 1) * 256],
                                        start=(kc == 0), stop=(kc == 15))
                    return inst
                p.op("pe", pvmm, reads=[rb("Ptb"), rb("Vb")], writes=[rpV()])
                p.op("dve",
                     lambda e, c, pq=pq, RScp=RScp: e.tensor_scalar(
                         Ohb[:, pq * 256:(pq + 1) * 256], PVp[:, :], svc(RScp), None, op0=ALU.mult),
                     reads=[rpV(), rsv(RScp)],
                     writes=[rb("Ohb", pq * 256, (pq + 1) * 256)])
            if qt is not None:
                for hf in range(2):
                    sl = (2 * qt + hf) % 3
                    sap, srg = S3AP[sl], S3RG[sl]
                    su = SUc[hf]
                    p.op("act",
                         lambda e, c, par=par, hf=hf, su=su, sap=sap, nmf=nmf: e.activation(
                             Pb[par][:, hf * 1024: hf * 1024 + 1024],
                             sap[:, :],
                             AF.Exp, bias=svc(nmf), scale=1.0, accum_out=svc(su)),
                         reads=[srg, rsv(nmf)],
                         writes=[rb(f"Pb{par}", hf * 1024, (hf + 1) * 1024), rsv(su)])
                p.op("dve", lambda e, c, SUc=SUc: e.tensor_tensor(
                    svc(SSUM), svc(SUc[0]), svc(SUc[1]), op=ALU.add),
                    reads=[rsv(SUc[0]), rsv(SUc[1])], writes=[rsv(SSUM)])
                p.op("dve", lambda e, c, RSc=RSc: e.reciprocal(svc(RSc), svc(SSUM)),
                     reads=[rsv(SSUM)], writes=[rsv(RSc)])
        grp = h // 2
        p.op("sync",
             lambda e, c, h=h, grp=grp: e.dma_start(
                 out=cin_d[grp][:, (h % 2) * HD:(h % 2 + 1) * HD].rearrange("(qt p) f -> p qt f", p=128),
                 in_=Ohb[:, :].rearrange("p (qt f) -> p qt f", f=HD)),
             reads=[rb("Ohb")], writes=[rdram(f"cin{grp}", 0, T)])
        if h % 2 == 1:
            p.op("pool", lambda e, c, grp=grp: e.collective_compute(
                "AllGather", ALU.bypass,
                replica_groups=[[0, 1], [2, 3], [4, 5], [6, 7]],
                ins=[cin_d[grp]], outs=[cout_d[grp]]),
                reads=[rdram(f"cin{grp}", 0, T)],
                writes=[rdram(f"cout{grp}", 0, 2 * T)], sem="cc")
            if h == NHC - 1:
                p.op("pool", lambda e, c: e.collective_compute(
                    "AllGather", ALU.bypass,
                    replica_groups=[[0, 1], [2, 3], [4, 5], [6, 7]],
                    ins=[dcin_d], outs=[dcout_d]),
                    reads=[("d:dcin", 0, 128)],
                    writes=[rdram("cout0", 0, 2 * T), rdram("cout1", 0, 2 * T),
                            rdram("cout2", 0, 2 * T), rdram("cout3", 0, 2 * T),
                            ("d:dcout", 0, 256)], sem="cc")

    p.op("pool", lambda e, c: e.dma_start(out=wrb[:, :], in_=wrp_d[:, :]),
         reads=[rdram("wrp", 0, 128)], writes=[rb("wrb")], sem="qgp")
    p.op("pool", lambda e, c: e.dma_start(out=ebb[:, :], in_=ebm_d[:, :]),
         reads=[rdram("ebm", 0, E)], writes=[rb("ebb")], sem="qgp")
    p.op("sync", lambda e, c: e.dma_start(out=bvsb[:, :], in_=bv_d[:, :]),
         reads=[rdram("bv", 0, 1)], writes=[rb("bvsb")])
    p.op("sync", lambda e, c: e.dma_start(out=rbs[:, :], in_=rbv_d[:, :]),
         reads=[rdram("rbv", 0, 1)], writes=[rb("rbs")])
    for j in range(8):
        bank = PROJ[j % 2]
        p.op("pe", lambda e, c, j=j, bank=bank: e.matmul(
            bank[:, :], onesb[0:1, :], bvsb[0:1, j * 512:(j + 1) * 512],
            start=True, stop=True),
            reads=[rb("onesb"), rb("bvsb", j * 512, (j + 1) * 512)],
            writes=[rpBank(j)])
        p.op("dve", lambda e, c, j=j, bank=bank: e.tensor_copy(
            bvB[:, j * 512:(j + 1) * 512], bank[:, :]),
            reads=[rpBank(j)], writes=[rb("bvB", j * 512, (j + 1) * 512)])
    p.op("pe", lambda e, c: e.matmul(PA[:, 0:64], onesb[0:1, :], rbs[0:1, :],
                                     start=True, stop=True),
         reads=[rb("onesb"), rb("rbs")], writes=[rpA(0, 64)])
    p.op("dve", lambda e, c: e.tensor_copy(brB[:, :], PA[:, 0:64]),
         reads=[rpA(0, 64)], writes=[rb("brB")])

    SVP = [{"MEAN": 0, "VAR": 1, "RSTD": 2, "NB": 3, "LNV": 4,
            "M8": 15, "SE": 16, "RSE": 17, "NMX": 18, "WS": 19},
           {"MEAN": 20, "VAR": 21, "RSTD": 22, "NB": 23, "LNV": 24,
            "M8": 25, "SE": 26, "RSE": 27, "NMX": 28, "WS": 29}]
    statsP = [stats, statsB]
    statsN = ["stats", "statsB"]
    for t in range(NTC):
        par = t % 2
        C = SVP[par]
        stt, stn = statsP[par], statsN[par]
        abhull = ("sb", 32768 + par * 16384, 32768 + (par + 1) * 16384)

        def sva(k, C=C):
            return svc(C[k])

        def rsva(k, C=C):
            return rsv(C[k])
        p.op("sync", lambda e, c, t=t, par=par: e.dma_start(
            out=xc[par][:, :], in_=xh_d[t * 128:(t + 1) * 128, :]),
            reads=[rdram("xh", t * 128, (t + 1) * 128)], writes=[rb(f"xc{par}")])
        for grp in range(4):
            p.op("pool", lambda e, c, par=par, t=t, grp=grp: e.dma_start(
                out=abP[par].rearrange("p (i f) -> p i f", f=2048)[:, :, grp * 512:(grp + 1) * 512],
                in_=cout_d[grp].rearrange("(i tt r) f -> tt r i f", i=4, tt=8, r=128)[t]),
                reads=[rdram(f"cout{grp}", 0, 2 * T)],
                writes=[("sb", 32768 + par * 16384 + grp * 1024,
                         32768 + par * 16384 + 16384)], sem="qgp")
        r1 = r1b[par]
        r1n = f"r1b{par}"
        for half in range(2):
            a0 = ab[par][2 * half + 0]
            a1 = ab[par][2 * half + 1]
            p.op("dve", lambda e, c, half=half, a0=a0, par=par, r1=r1: e.scalar_tensor_tensor(
                r1[:, half * 2048:(half + 1) * 2048], a0[:, :], mskb[:, 0:1],
                xc[par][:, half * 2048:(half + 1) * 2048], op0=ALU.mult, op1=ALU.add),
                reads=[abhull, rb("mskb", 0, 1), rb(f"xc{par}", half * 2048, (half + 1) * 2048)],
                writes=[rb(r1n, half * 2048, (half + 1) * 2048)])
            p.op("dve", lambda e, c, half=half, a1=a1, r1=r1: e.scalar_tensor_tensor(
                r1[:, half * 2048:(half + 1) * 2048], a1[:, :], mskb[:, 1:2],
                r1[:, half * 2048:(half + 1) * 2048], op0=ALU.mult, op1=ALU.add),
                reads=[abhull, rb("mskb", 1, 2), rb(r1n, half * 2048, (half + 1) * 2048)],
                writes=[rb(r1n, half * 2048, (half + 1) * 2048)])
            p.op("dve", lambda e, c, half=half, r1=r1: e.tensor_tensor(
                r1[:, half * 2048:(half + 1) * 2048],
                r1[:, half * 2048:(half + 1) * 2048],
                bvB[:, half * 2048:(half + 1) * 2048], op=ALU.add),
                reads=[rb(r1n, half * 2048, (half + 1) * 2048),
                       rb("bvB", half * 2048, (half + 1) * 2048)],
                writes=[rb(r1n, half * 2048, (half + 1) * 2048)])
        for cch in range(8):
            p.op("dve", lambda e, c, cch=cch, r1=r1, stt=stt: e.bn_stats(
                stt[:, cch * 6:cch * 6 + 6], r1[:, cch * 512:(cch + 1) * 512]),
                reads=[rb(r1n, cch * 512, (cch + 1) * 512)],
                writes=[rb(stn, cch * 6, cch * 6 + 6)])
        p.op("dve", lambda e, c, stt=stt: e.bn_aggr(sv[:, C["MEAN"]:C["MEAN"] + 2], stt[:, :]),
             reads=[rb(stn)], writes=[rb("sv", C["MEAN"], C["VAR"] + 1)])
        p.op("dve", lambda e, c: e.tensor_scalar(sva("LNV"), sva("VAR"), EPS, None, op0=ALU.add),
             reads=[rsva("VAR")], writes=[rsva("LNV")])
        p.op("act", lambda e, c: e.activation(sva("LNV"), sva("LNV"), AF.Ln),
             reads=[rsva("LNV")], writes=[rsva("LNV")])
        p.op("act", lambda e, c: e.activation(sva("RSTD"), sva("LNV"), AF.Exp, scale=-0.5),
             reads=[rsva("LNV")], writes=[rsva("RSTD")])
        p.op("dve", lambda e, c: e.tensor_scalar(sva("NB"), sva("MEAN"), -1.0, None, op0=ALU.mult),
             reads=[rsva("MEAN")], writes=[rsva("NB")])
        lt = ln2tm[par]
        ltn = f"ln2tm{par}"
        lf = ln2fm[par]
        lfn = f"ln2fm{par}"
        for half in range(2):
            p.op("act", lambda e, c, half=half, r1=r1, lt=lt: e.activation(
                lt[:, half * 2048:(half + 1) * 2048],
                r1[:, half * 2048:(half + 1) * 2048],
                AF.Identity, bias=sva("NB"), scale=sva("RSTD")),
                reads=[rb(r1n, half * 2048, (half + 1) * 2048), rsva("NB"), rsva("RSTD")],
                writes=[rb(ltn, half * 2048, (half + 1) * 2048)])
        for g in range(4):
            def tr2group(e, c, g=g, lt=lt):
                inst = None
                for j in range(8):
                    inst = e.transpose(
                        Tps[:, j * 128:(j + 1) * 128],
                        lt[:, (g * 8 + j) * 128:(g * 8 + j) * 128 + 128],
                        identb[:, :])
                return inst
            p.op("pe", tr2group,
                 reads=[rb(ltn, g * 1024, (g + 1) * 1024), rb("identb")],
                 writes=[rpT()])
            eng = "dve" if g % 2 == 0 else "act"
            fnc = (lambda e, c, g=g, lf=lf: e.tensor_copy(
                lf.rearrange("p (kc t1) -> p kc t1", t1=128)[:, g * 8:(g + 1) * 8, :],
                Tps[:, 0:1024].rearrange("p (kc t1) -> p kc t1", t1=128))) if eng == "dve" else (
                lambda e, c, g=g, lf=lf: e.copy(
                    lf.rearrange("p (kc t1) -> p kc t1", t1=128)[:, g * 8:(g + 1) * 8, :],
                    Tps[:, 0:1024].rearrange("p (kc t1) -> p kc t1", t1=128)))
            p.op(eng, fnc, reads=[rpT()], writes=[rb(lfn)])
        def rmm(e, c, lf=lf):
            inst = None
            for kc in range(KC):
                inst = e.matmul(PA[:, 0:64], lf[:, kc * 128:(kc + 1) * 128],
                                wrb[:, kc * 64:(kc + 1) * 64],
                                start=(kc == 0), stop=(kc == 31))
            return inst
        p.op("pe", rmm, reads=[rb(lfn), rb("wrb")], writes=[rpA(0, 64)])
        sc, scn = scb[par], f"scb{par}"
        ex, exn = exb[par], f"exb{par}"
        sw, swn = swb[par], f"swb{par}"
        wf, wfn = wdf[par], f"wdf{par}"
        wm, wmn = wdm[par], f"wdm{par}"
        wb_, wbn = wdb[par], f"wdb{par}"
        wt, wtn = wdnT[par], f"wdnT{par}"
        p.op("dve", lambda e, c, sc=sc: e.tensor_tensor(
            sc[:, :], PA[:, 0:64], brB[:, :], op=ALU.add),
            reads=[rpA(0, 64), rb("brB")], writes=[rb(scn)])
        p.op("dve", lambda e, c, sc=sc: e.tensor_reduce(
            sva("NMX"), sc[:, :], axis=AX.X, op=ALU.max, negate=True),
            reads=[rb(scn)], writes=[rsva("NMX")])
        p.op("act", lambda e, c, sc=sc, ex=ex: e.activation(
            ex[:, :], sc[:, :], AF.Exp, bias=sva("NMX"), scale=1.0, accum_out=sva("SE")),
            reads=[rb(scn), rsva("NMX")], writes=[rb(exn), rsva("SE")])
        p.op("dve", lambda e, c: e.reciprocal(sva("RSE"), sva("SE")),
             reads=[rsva("SE")], writes=[rsva("RSE")])
        p.op("dve", lambda e, c, sc=sc, ex=ex: e.tensor_scalar(
            sc[:, :], ex[:, :], sva("RSE"), None, op0=ALU.mult),
            reads=[rb(exn), rsva("RSE")], writes=[rb(scn)])
        p.op("dve", lambda e, c, sc=sc, sw=sw: e.tensor_copy(sw[:, :], sc[:, :]),
             reads=[rb(scn)], writes=[rb(swn)])
        for j in range(KTOP):
            p.op("dve", lambda e, c, sw=sw: e.tensor_reduce(
                sva("M8"), sw[:, :], axis=AX.X, op=ALU.max),
                reads=[rb(swn)], writes=[rsva("M8")])
            if j < KTOP - 1:
                p.op("dve", lambda e, c, sw=sw, wm=wm: e.tensor_scalar(
                    wm[:, :], sw[:, :], sva("M8"), -2.0, op0=ALU.is_ge, op1=ALU.mult),
                    reads=[rb(swn), rsva("M8")], writes=[rb(wmn)])
                p.op("dve", lambda e, c, sw=sw, wm=wm: e.tensor_tensor(
                    sw[:, :], sw[:, :], wm[:, :], op=ALU.add),
                    reads=[rb(swn), rb(wmn)], writes=[rb(swn)])
        p.op("dve", lambda e, c, sc=sc, wf=wf: e.scalar_tensor_tensor(
            wf[:, :], sc[:, :], sva("M8"), sc[:, :], op0=ALU.is_ge, op1=ALU.mult,
            accum_out=sva("WS")),
            reads=[rb(scn), rsva("M8")], writes=[rb(wfn), rsva("WS")])
        p.op("dve", lambda e, c, wf=wf, wb_=wb_: e.tensor_copy(wb_[:, :], wf[:, :]),
             reads=[rb(wfn)], writes=[rb(wbn)])
        p.op("pe", lambda e, c, wb_=wb_: e.transpose(
            Tps[0:64, 0:128], wb_[:, 0:64], identb[:, :]),
            reads=[rb(wbn), rb("identb")], writes=[rpT(0, 128)])
        p.op("act", lambda e, c, wt=wt: e.copy(wt[0:E, :], Tps[0:64, 0:128]),
             reads=[rpT(0, 128)], writes=[rb(wtn)])
        for j in range(8):
            bank = PROJ[(j + 1) % 2]
            p.op("pe", lambda e, c, j=j, bank=bank, wt=wt: e.matmul(
                bank[:, :], wt[0:E, :], ebb[0:E, j * 512:(j + 1) * 512],
                start=True, stop=True),
                reads=[rb(wtn), rb("ebb", j * 512, (j + 1) * 512)],
                writes=[rpBank(j + 1)])
            p.op("dve", lambda e, c, j=j, bank=bank, r1=r1: e.tensor_tensor(
                r1[:, j * 512:(j + 1) * 512], bank[:, :],
                r1[:, j * 512:(j + 1) * 512], op=ALU.add),
                reads=[rpBank(j + 1), rb(r1n, j * 512, (j + 1) * 512)],
                writes=[rb(r1n, j * 512, (j + 1) * 512)])
        p.op("dve", lambda e, c, r1=r1, lt=lt: e.scalar_tensor_tensor(
            r1[:, :], lt[:, :], sva("WS"), r1[:, :], op0=ALU.mult, op1=ALU.add),
            reads=[rb(ltn), rsva("WS"), rb(r1n)], writes=[rb(r1n)])
        p.op("sync", lambda e, c, t=t, par=par, r1=r1: e.dma_start(
            out=y_d[t * 128:(t + 1) * 128, :], in_=r1[:, :]),
            reads=[rb(r1n)], writes=[rdram("y", t * 128, (t + 1) * 128)])

    p.items["sync"].append(("w", "qsy", [p.cnt["qsy"]]))
    ops = p.finalize()

    with (
        nc.semaphore("s_pe") as s_pe,
        nc.semaphore("s_dve") as s_dve,
        nc.semaphore("s_act") as s_act,
        nc.semaphore("s_pool") as s_pool,
        nc.semaphore("s_qsy") as s_qsy,
        nc.semaphore("s_qsc") as s_qsc,
        nc.semaphore("s_qgp") as s_qgp,
        nc.semaphore("s_cc") as s_cc,
        nc.Block() as block,
    ):
        ctx = {"pe": s_pe, "dve": s_dve, "act": s_act, "pool": s_pool,
               "qsy": s_qsy, "qsc": s_qsc, "qgp": s_qgp, "cc": s_cc}

        @block.tensor
        def _(e):
            for f in ops["pe"]:
                f(e, ctx)

        @block.vector
        def _(e):
            for f in ops["dve"]:
                f(e, ctx)

        @block.scalar
        def _(e):
            for f in ops["act"]:
                f(e, ctx)

        @block.gpsimd
        def _(e):
            for f in ops["pool"]:
                f(e, ctx)

        @block.sync
        def _(e):
            for f in ops["sync"]:
                f(e, ctx)

    return nc


_NC_CACHE = None


def _get_nc():
    global _NC_CACHE
    if _NC_CACHE is None:
        _NC_CACHE = build_nc()
    return _NC_CACHE


def kernel(L0_attn_x, L0_attn_Wq, L0_attn_bq, L0_attn_Wk, L0_attn_bk, L0_attn_Wv, L0_attn_bv,
           L0_attn_bias_weight, L0_ffn_router_weight, L0_ffn_router_bias, L0_ffn_expert_bias):
    x = np.asarray(L0_attn_x, dtype=np.float32)
    Wq = np.asarray(L0_attn_Wq, dtype=np.float32)
    Wk = np.asarray(L0_attn_Wk, dtype=np.float32)
    Wv = np.asarray(L0_attn_Wv, dtype=np.float32)
    bq = np.asarray(L0_attn_bq, dtype=np.float32)
    bk = np.asarray(L0_attn_bk, dtype=np.float32)
    bv = np.asarray(L0_attn_bv, dtype=np.float32)
    bw = np.asarray(L0_attn_bias_weight, dtype=np.float32)
    Wr = np.asarray(L0_ffn_router_weight, dtype=np.float32)
    rb_ = np.asarray(L0_ffn_router_bias, dtype=np.float32)
    eb = np.asarray(L0_ffn_expert_bias, dtype=np.float32)

    bdp = _host_constants()
    ident = np.eye(128, dtype=np.float32)
    onesv = np.ones((1, 128), dtype=np.float32)
    wrp = np.ascontiguousarray(Wr.T.reshape(KC, 128, E).transpose(1, 0, 2).reshape(128, KC * E))

    def pack_w(W, s):
        WT = W.T[:, s * 2048:(s + 1) * 2048]
        out = np.empty((NHC, 4, 128, 2048), dtype=np.float32)
        for j in range(NHC):
            sl = WT[:, j * HD:(j + 1) * HD]
            a = sl.reshape(4, 8, 128, HD).transpose(0, 2, 1, 3)
            out[j] = a.reshape(4, 128, 8 * HD)
        return out

    in_maps = []
    for core in range(8):
        b, s = divmod(core, 2)
        wpk = np.stack([pack_w(Wq, s), pack_w(Wk, s), pack_w(Wv, s)], axis=1)
        bqs = bq[s * 2048:(s + 1) * 2048].reshape(NHC, 2, 128).transpose(2, 0, 1).reshape(128, 16)
        bks = bk[s * 2048:(s + 1) * 2048].reshape(NHC, 2, 128).transpose(2, 0, 1).reshape(128, 16)
        bws = np.broadcast_to(
            bw[s * NHC:(s + 1) * NHC].T.reshape(3, 1, NHC, 1), (3, 128, NHC, 128)
        ).reshape(3, 128, NHC * 128)
        msk = np.zeros((128, 2), dtype=np.float32)
        msk[:, s] = 1.0
        in_maps.append({
            "x": np.ascontiguousarray(x[b]),
            "xh": np.ascontiguousarray(x[b, s * TC:(s + 1) * TC]),
            "wpk": np.ascontiguousarray(wpk),
            "bqp": np.ascontiguousarray(bqs),
            "bkp": np.ascontiguousarray(bks),
            "bv": bv.reshape(1, D),
            "bdp": bdp,
            "bws": np.ascontiguousarray(bws),
            "wrp": wrp,
            "rbv": rb_.reshape(1, E),
            "ebm": eb,
            "idm": ident,
            "onesv": onesv,
            "msk": msk,
        })

    nc = _get_nc()
    res = run_bass_kernel_spmd(nc, in_maps, core_ids=list(range(8)))
    out = np.empty((B, L, D), dtype=np.float32)
    for core in range(8):
        b, s = divmod(core, 2)
        out[b, s * TC:(s + 1) * TC, :] = res.results[core]["y"]
    return out
